# version 04
# baseline (speedup 1.0000x reference)
import numpy as np
import ml_dtypes
import concourse.bass as bass
import concourse.mybir as mybir
from concourse.bass_utils import run_bass_kernel_spmd
from contextlib import ExitStack

F32 = mybir.dt.float32
BF16 = mybir.dt.bfloat16
AF = mybir.ActivationFunctionType
ALU = mybir.AluOpType
AX = mybir.AxisListType

D = 1024
DI = 2048
SEQ = 2048
NB = 2
NTT = 16
NE = 8
IE = 1408
NI = 11
EPS = 1e-5

PHASES = ("p0", "p1", "p2", "p3a", "p3b")
DEBUG = False


class Prog:
    ENGS = ("te", "ve", "se", "po", "sy")

    def __init__(self, nc, sems):
        self.nc = nc
        self.sem = sems
        self.ops = {e: [] for e in self.ENGS}
        self.c = {k: 0 for k in sems}

    def w(self, eng, sem, val):
        if val > 0:
            self.ops[eng].append(lambda E, s=self.sem[sem], v=val: E.wait_ge(s, v))

    def te(self, fn, inc=True):
        if inc:
            self.c["te"] += 1
            n = self.c["te"]
            self.ops["te"].append(lambda E, f=fn, s=self.sem["te"]: f(E).then_inc(s, 1))
            return n
        self.ops["te"].append(fn)
        return self.c["te"]

    def _serial(self, eng, fn):
        prev = self.c[eng]
        if prev > 0:
            self.ops[eng].append(lambda E, s=self.sem[eng], v=prev: E.wait_ge(s, v))
        self.c[eng] += 1
        n = self.c[eng]
        self.ops[eng].append(lambda E, f=fn, s=self.sem[eng]: f(E).then_inc(s, 1))
        return n

    def ve(self, fn):
        return self._serial("ve", fn)

    def se(self, fn):
        return self._serial("se", fn)

    def dma(self, q, sem, fn):
        eng = {"qs": "sy", "qa": "se", "qp": "po"}[q]
        prev = self.c[sem]
        if prev > 0:
            self.ops[eng].append(lambda E, s=self.sem[sem], v=prev: E.wait_ge(s, v))
        self.c[sem] += 16
        n = self.c[sem]
        self.ops[eng].append(lambda E, f=fn, s=self.sem[sem]: f(E).then_inc(s, 16))
        return n


def build(debug=DEBUG, phases=PHASES):
    nc = bass.Bass("TRN2", target_bir_lowering=False)

    x = nc.dram_tensor("x", [NB, SEQ, D], F32, kind="ExternalInput").ap()
    w_in = nc.dram_tensor("w_in", [D, 2 * DI], F32, kind="ExternalInput").ap()
    b_in2 = nc.dram_tensor("b_in2", [128, 32], F32, kind="ExternalInput").ap()
    conv_w2 = nc.dram_tensor("conv_w2", [128, 64], F32, kind="ExternalInput").ap()
    conv_b2 = nc.dram_tensor("conv_b2", [128, 16], F32, kind="ExternalInput").ap()
    w_dt = nc.dram_tensor("w_dt", [DI, DI], F32, kind="ExternalInput").ap()
    b_dt2 = nc.dram_tensor("b_dt2", [128, 16], F32, kind="ExternalInput").ap()
    dsk2 = nc.dram_tensor("dsk2", [128, 16], F32, kind="ExternalInput").ap()
    w_out = nc.dram_tensor("w_out", [DI, D], F32, kind="ExternalInput").ap()
    b_out_b = nc.dram_tensor("b_out_b", [128, D], F32, kind="ExternalInput").ap()
    ln_g_b = nc.dram_tensor("ln_g_b", [128, D], F32, kind="ExternalInput").ap()
    ln_b_b = nc.dram_tensor("ln_b_b", [128, D], F32, kind="ExternalInput").ap()
    wr2 = nc.dram_tensor("wr2", [128, 64], F32, kind="ExternalInput").ap()
    wg = nc.dram_tensor("wg", [NE, D, IE], F32, kind="ExternalInput").ap()
    wu = nc.dram_tensor("wu", [NE, D, IE], F32, kind="ExternalInput").ap()
    wd = nc.dram_tensor("wd", [NE, IE, D], F32, kind="ExternalInput").ap()
    wg_s = nc.dram_tensor("wg_s", [D, IE], F32, kind="ExternalInput").ap()
    wu_s = nc.dram_tensor("wu_s", [D, IE], F32, kind="ExternalInput").ap()
    wd_s = nc.dram_tensor("wd_s", [IE, D], F32, kind="ExternalInput").ap()
    ident = nc.dram_tensor("ident", [128, 128], BF16, kind="ExternalInput").ap()

    out = nc.dram_tensor("out", [NB, SEQ, D], F32, kind="ExternalOutput").ap()

    xnT_sp = nc.dram_tensor("xnT_sp", [NB, 128, 8, SEQ], BF16, kind="Internal").ap()
    g_sp = nc.dram_tensor("g_sp", [NB, NTT, 128, D], BF16, kind="Internal").ap()
    xs_sp = nc.dram_tensor("xs_sp", [NB, 16, 128, SEQ], F32, kind="Internal").ap()
    xsbf_sp = nc.dram_tensor("xsbf_sp", [NB, 128, 16, SEQ], BF16, kind="Internal").ap()
    res_sp = nc.dram_tensor("res_sp", [NB, 16, 128, SEQ], F32, kind="Internal").ap()
    ybf_sp = nc.dram_tensor("ybf_sp", [NB, 128, 16, SEQ], BF16, kind="Internal").ap()

    dbg = {}
    if debug:
        def dbg_t(name, shape, dt):
            dbg[name] = nc.dram_tensor("dbg_" + name, shape, dt, kind="ExternalOutput").ap()
        dbg_t("xnT", [NB, 128, 8, SEQ], BF16)
        dbg_t("cw", [128, 256], F32)
        dbg_t("g", [NB, NTT, 128, D], BF16)
        dbg_t("xs", [NB, 16, 128, SEQ], F32)
        dbg_t("res", [NB, 16, 128, SEQ], F32)
        dbg_t("ybf", [NB, 128, 16, SEQ], BF16)

    with ExitStack() as stk:
        sb = lambda name, shape, dt=F32: stk.enter_context(nc.sbuf_tensor(name, shape, dt))
        sems = {}
        DMA_SEMS = ("s_ld", "s_w", "s_w1", "s_w2", "s_w3", "s_x0", "s_x1", "s_sp0", "s_sp1",
                    "s_wg0", "s_wg1", "s_wg2", "s_wg3", "s_wd", "s_gb0", "s_gb1",
                    "s_xsf0", "s_xsf1", "s_xsb0", "s_xsb1", "s_rv0", "s_rv1",
                    "s_xf0", "s_xf1", "s_rs", "s_ybf0", "s_ybf1",
                    "s_g0", "s_g1", "s_o0", "s_o1")
        for s in ("te", "ve", "se") + DMA_SEMS:
            sems[s] = stk.enter_context(nc.semaphore("sem_" + s))
        p = Prog(nc, sems)

        pf = stk.enter_context(nc.psum_tensor("pf", [128, 3072], F32))
        pbf = [stk.enter_context(nc.psum_tensor(f"pbf{i}", [128, 1024], BF16)) for i in range(2)]
        PAIR = [pf[:, k * 1024:(k + 1) * 1024] for k in range(3)]

        arA = sb("arA", [128, 16384])
        arB = sb("arB", [128, 16384])
        arC = sb("arC", [128, 8192])
        arD = sb("arD", [128, 9728])

        def f32v(ar, off, n):
            assert off % 4 == 0
            return ar[:, off // 4: off // 4 + n]

        def bf16v(ar, off, n):
            assert off % 2 == 0
            return ar[:].bitcast(BF16)[:, off // 2: off // 2 + n]

        ident_sb = sb("ident_sb", [128, 128], BF16)
        wr_sb = sb("wr_sb", [128, 64], BF16)
        biases = sb("biases", [128, 32])
        convw_sb = sb("convw_sb", [128, 64])
        convb_sb = sb("convb_sb", [128, 16])
        bdt_sb = sb("bdt_sb", [128, 16])
        dsk_sb = sb("dsk_sb", [128, 16])
        cw_all = sb("cw_all", [128, 32 * 8])
        ln1s = sb("ln1s", [128, 64])
        c_eps = sb("c_eps", [128, 1])
        c_two = sb("c_two", [128, 1])
        c_one = sb("c_one", [128, 1])
        c_zero = sb("c_zero", [128, 1])
        ones8 = sb("ones8", [128, 8])

        for dst, src in ((ident_sb, ident), (biases, b_in2), (convw_sb, conv_w2),
                         (convb_sb, conv_b2), (bdt_sb, b_dt2), (dsk_sb, dsk2)):
            p.dma("qs", "s_ld", lambda E, d=dst, s=src: E.dma_start(out=d[:], in_=s))
        n_const = p.c["s_ld"]
        n_wr = p.dma("qp", "s_w", lambda E: E.dma_start(out=wr_sb[:], in_=wr2))
        for buf, v in ((c_eps, EPS), (c_two, 2.0), (c_one, 1.0), (c_zero, 0.0), (ones8, 1.0)):
            p.ve(lambda E, b=buf, v=v: E.memset(b[:], v))

        def barrier():
            snap = {kk: vv for kk, vv in p.c.items() if vv > 0}
            for eng in ("te", "ve", "se", "sy", "po"):
                for ss, vv in snap.items():
                    p.w(eng, ss, vv)

        if "p0" in phases:
            xtile = [f32v(arD, 0, 1024), f32v(arD, 4096, 1024)]
            xnbf = [bf16v(arD, 8192, 1024), bf16v(arD, 10240, 1024)]
            stage = [bf16v(arD, 12288, 1024), bf16v(arD, 14336, 1024)]
            bnst = f32v(arD, 16384, 12)
            bnag = f32v(arD, 16448, 2)
            rstd0 = f32v(arD, 16456, 1)
            tln = f32v(arD, 16460, 1)
            off = 16464
            smalls = {}
            for nm in ("sc8", "e8", "eq1", "cum8", "ile8", "msk1", "sv2", "eq2", "mskT", "mskd"):
                smalls[nm] = f32v(arD, off, 8); off += 32
            for nm in ("m1", "nm1", "m2", "den8", "rden"):
                smalls[nm] = f32v(arD, off, 1); off += 4
            sc8 = smalls["sc8"]; e8 = smalls["e8"]; eq1 = smalls["eq1"]; cum8 = smalls["cum8"]
            ile8 = smalls["ile8"]; msk1 = smalls["msk1"]; sv2 = smalls["sv2"]; eq2 = smalls["eq2"]
            mskT = smalls["mskT"]; mskd = smalls["mskd"]; m1 = smalls["m1"]; nm1 = smalls["nm1"]
            m2 = smalls["m2"]; den8 = smalls["den8"]; rden = smalls["rden"]

            p.w("te", "s_ld", n_const)
            p.w("te", "s_w", n_wr)

            xload = [0, 0]
            xfree = [0, 0]
            stfree = [(0, 0), (0, 0)]
            ptfree = [0, 0]
            prfree = 0
            xnbf_free = [0, 0]

            for tt in range(32):
                b, tl = divmod(tt, NTT)
                buf = tt % 2
                p.w("sy", "ve", xfree[buf])
                xload[buf] = p.dma("qs", "s_x%d" % buf, lambda E, b=b, tl=tl, d=xtile[buf]:
                    E.dma_start(out=d, in_=x[b, tl * 128:(tl + 1) * 128, :]))
                p.w("ve", "s_x%d" % buf, xload[buf])
                p.ve(lambda E, d=xtile[buf]: E.bn_stats(bnst[:, 0:6], d[:, 0:512]))
                p.ve(lambda E, d=xtile[buf]: E.bn_stats(bnst[:, 6:12], d[:, 512:1024]))
                n_ag = p.ve(lambda E: E.bn_aggr(bnag, bnst))
                p.w("se", "ve", n_ag)
                p.se(lambda E: E.activation(tln, bnag[:, 1:2], AF.Ln, bias=c_eps[:], scale=1.0))
                n_rs = p.se(lambda E: E.activation(rstd0, tln, AF.Exp, bias=c_zero[:], scale=-0.5))
                p.w("ve", "se", n_rs)
                p.ve(lambda E, tt=tt: E.tensor_copy(ln1s[:, tt * 2:tt * 2 + 1], bnag[:, 0:1]))
                p.ve(lambda E, tt=tt: E.tensor_copy(ln1s[:, tt * 2 + 1:tt * 2 + 2], rstd0))
                p.w("ve", "te", xnbf_free[buf])
                n_xn = p.ve(lambda E, d=xtile[buf], o=xnbf[buf]:
                    E.tensor_scalar(o, d, bnag[:, 0:1], rstd0, ALU.subtract, ALU.mult))
                xfree[buf] = n_xn
                p.w("te", "ve", n_xn)
                p.w("te", "ve", ptfree[buf])
                for dd in range(8):
                    n_tr = p.te(lambda E, dd=dd, o=pbf[buf], i=xnbf[buf]:
                        E.transpose(o[:, dd * 128:(dd + 1) * 128], i[:, dd * 128:(dd + 1) * 128], ident_sb[:]),
                        inc=(dd == 7))
                xnbf_free[buf] = n_tr
                p.w("ve", "te", n_tr)
                p.w("ve", "s_sp%d" % buf, stfree[buf][0])
                p.w("ve", "te", stfree[buf][1])
                n_ev = p.ve(lambda E, o=stage[buf], i=pbf[buf]: E.tensor_copy(o, i[:]))
                ptfree[buf] = n_ev
                p.w("se", "ve", n_ev)
                n_sp = p.dma("qa", "s_sp%d" % buf, lambda E, b=b, tl=tl, i=stage[buf]:
                    E.dma_start(out=xnT_sp[b, :, :, tl * 128:(tl + 1) * 128],
                                in_=i.rearrange("p (d c) -> p d c", d=8)))
                p.w("te", "ve", n_ev)
                p.w("te", "ve", prfree)
                for dd in range(8):
                    n_rt = p.te(lambda E, dd=dd, i=stage[buf]:
                        E.matmul(PAIR[0][:, 0:8], i[:, dd * 128:(dd + 1) * 128], wr_sb[:, dd * 8:(dd + 1) * 8],
                                 start=(dd == 0), stop=(dd == 7)), inc=(dd == 7))
                stfree[buf] = (n_sp, n_rt)
                p.w("ve", "te", n_rt)
                n_cp = p.ve(lambda E: E.tensor_copy(sc8, PAIR[0][:, 0:8]))
                prfree = n_cp
                p.ve(lambda E: E.tensor_reduce(m1, sc8, AX.X, ALU.max))
                n_m1 = p.ve(lambda E: E.tensor_scalar(nm1, m1, -1.0, None, ALU.mult))
                p.w("se", "ve", n_m1)
                n_e8 = p.se(lambda E: E.activation(e8, sc8, AF.Exp, bias=nm1, scale=1.0))
                p.ve(lambda E: E.tensor_scalar(eq1, sc8, m1, None, ALU.is_ge))
                p.ve(lambda E: E.tensor_tensor_scan(cum8, ones8[:], eq1, c_zero[:], ALU.mult, ALU.add))
                p.ve(lambda E: E.tensor_scalar(ile8, cum8, 1.5, None, ALU.is_le))
                p.ve(lambda E: E.tensor_tensor(msk1, eq1, ile8, ALU.mult))
                p.ve(lambda E: E.scalar_tensor_tensor(sv2, msk1, -1e30, sc8, ALU.mult, ALU.add))
                p.ve(lambda E: E.tensor_reduce(m2, sv2, AX.X, ALU.max))
                p.ve(lambda E: E.tensor_scalar(eq2, sv2, m2, None, ALU.is_ge))
                p.ve(lambda E: E.tensor_tensor_scan(cum8, ones8[:], eq2, c_zero[:], ALU.mult, ALU.add))
                p.ve(lambda E: E.tensor_scalar(ile8, cum8, 1.5, None, ALU.is_le))
                p.ve(lambda E: E.tensor_tensor(mskT, eq2, ile8, ALU.mult))
                p.ve(lambda E: E.tensor_tensor(mskT, mskT, msk1, ALU.add))
                p.w("ve", "se", n_e8)
                p.ve(lambda E: E.tensor_tensor(mskd, mskT, e8, ALU.mult))
                p.ve(lambda E: E.tensor_reduce(den8, mskd, AX.X, ALU.add))
                p.ve(lambda E: E.reciprocal(rden, den8))
                p.ve(lambda E, tt=tt: E.tensor_scalar(cw_all[:, tt * 8:(tt + 1) * 8], mskd, rden, None, ALU.mult))

            if debug:
                pass
                p.w("sy", "ve", p.c["ve"])
                p.dma("qs", "s_ld", lambda E: E.dma_start(out=dbg["xnT"], in_=xnT_sp))
                p.dma("qs", "s_ld", lambda E: E.dma_start(out=dbg["cw"], in_=cw_all[:]))

        if "p1" in phases:
            barrier()
            y_acc = arA[:, :]
            h_e = bf16v(arB, 0, NI * SEQ)
            xn_b = bf16v(arC, 0, 8 * SEQ)
            wd_sb = bf16v(arD, 0, NI * 1024)
            wgu = bf16v(arD, 22528, 4 * 1024)
            sg_tmp = [f32v(arD, 30720, 1024), f32v(arD, 34816, 1024)]
            dn_tmp = [f32v(arB, 45056, 1024), f32v(arB, 49152, 1024)]
            gbf = [bf16v(arB, 53248, 1024), bf16v(arB, 55296, 1024)]

            E_ORDER = [8] + list(range(8))

            def wgu_src(e, i, which):
                if e == 8:
                    t = wg_s if which == 0 else wu_s
                    return t[:, i * 128:(i + 1) * 128].rearrange("(k p) m -> p k m", p=128)
                t = wg if which == 0 else wu
                return t[e][:, i * 128:(i + 1) * 128].rearrange("(k p) m -> p k m", p=128)

            def wd_src(e):
                t = wd_s if e == 8 else wd[e]
                return t.rearrange("(it p) f -> p it f", p=128)

            pair_free = [("ve", 0), ("ve", 0), ("ve", 0)]
            slot_load = [0, 0, 0, 0]
            slot_freeT = [0, 0, 0, 0]
            wd_load = 0
            wd_freeT = 0
            n_last_dn_mm = 0
            n_last_h_ev = 0
            n_se_shared = 0
            sg_free = [0, 0]
            dn_free = [0, 0]
            gb_free = [0, 0]
            units = [(e, i) for e in E_ORDER for i in range(NI)]
            NU = len(units)

            for b in range(NB):
                p.w("sy", "te", p.c["te"])
                n_xnb = p.dma("qs", "s_ld", lambda E, b=b: E.dma_start(
                    out=xn_b.rearrange("p (d t) -> p d t", d=8), in_=xnT_sp[b]))
                p.w("te", "s_ld", n_xnb)
                first_v_add_waits = 0

                def issue_wgu_dma(uidx):
                    e, i = units[uidx]
                    for which in (0, 1):
                        slot = (uidx * 2 + which) % 4
                        p.w("po", "te", slot_freeT[slot])
                        slot_load[slot] = p.dma("qp", "s_wg%d" % slot, lambda E, e=e, i=i, w=which, s=slot:
                            E.dma_start(out=wgu[:, s * 1024:(s + 1) * 1024].rearrange("p (k m) -> p k m", k=8),
                                        in_=wgu_src(e, i, w)))

                issue_wgu_dma(0)
                issue_wgu_dma(1)

                ucount = 0
                for uidx, (e, i) in enumerate(units):
                    for th in range(2):
                        gp = PAIR[(ucount * 2) % 3]
                        up = PAIR[(ucount * 2 + 1) % 3]
                        gslot = (uidx * 2) % 4
                        uslot = (uidx * 2 + 1) % 4
                        if th == 0:
                            p.w("te", "s_wg%d" % gslot, slot_load[gslot])
                        for pi, (pp, slot) in enumerate(((gp, gslot), (up, uslot))):
                            if th == 0 and pi == 1:
                                p.w("te", "s_wg%d" % uslot, slot_load[uslot])
                            eng, val = pair_free[(ucount * 2 + pi) % 3]
                            p.w("te", eng, val)
                            for k in range(8):
                                for sl in range(2):
                                    n_mm = p.te(lambda E, pp=pp, slot=slot, k=k, sl=sl, th=th:
                                        E.matmul(pp[:, sl * 512:(sl + 1) * 512],
                                                 wgu[:, slot * 1024 + k * 128: slot * 1024 + (k + 1) * 128],
                                                 xn_b[:, k * SEQ + th * 1024 + sl * 512: k * SEQ + th * 1024 + (sl + 1) * 512],
                                                 start=(k == 0), stop=(k == 7)),
                                        inc=(k == 7 and sl == 1))
                        if th == 1:
                            slot_freeT[gslot] = n_mm
                            slot_freeT[uslot] = n_mm
                        stb = sg_tmp[ucount % 2]
                        p.w("se", "te", n_mm)
                        p.w("se", "ve", sg_free[ucount % 2])
                        n_sg = p.se(lambda E, pp=gp, o=stb: E.activation(o, pp[:], AF.Silu))
                        p.w("ve", "se", n_sg)
                        p.w("ve", "te", n_last_dn_mm)
                        n_h = p.ve(lambda E, pp=up, o=h_e, i=i, th=th, stb=stb:
                            E.tensor_tensor(o[:, i * SEQ + th * 1024: i * SEQ + (th + 1) * 1024],
                                            stb, pp[:], ALU.mult))
                        pair_free[(ucount * 2) % 3] = ("se", n_sg)
                        pair_free[(ucount * 2 + 1) % 3] = ("ve", n_h)
                        sg_free[ucount % 2] = n_h
                        n_last_h_ev = n_h
                        ucount += 1
                    if uidx + 2 < NU:
                        issue_wgu_dma(uidx + 2)
                    if 4 <= i <= 7:
                        ch = i - 4
                        lo, hi = (3 * ch, 3 * ch + 3) if ch < 3 else (9, 11)
                        if ch == 0:
                            p.w("po", "te", wd_freeT)
                        wd_load = p.dma("qp", "s_wd", lambda E, e=e, lo=lo, hi=hi:
                            E.dma_start(out=wd_sb[:, lo * 1024:hi * 1024].rearrange("p (it f) -> p it f", f=1024),
                                        in_=wd_src(e)[:, lo:hi, :]))
                    if i == NI - 1:
                        p.w("te", "s_wd", wd_load)
                        p.w("te", "ve", n_last_h_ev)
                        for tt in range(16):
                            dpair = PAIR[ucount % 3]
                            eng, val = pair_free[ucount % 3]
                            p.w("te", eng, val)
                            for ii in range(NI):
                                for fh in range(2):
                                    n_dmm = p.te(lambda E, dpair=dpair, ii=ii, tt=tt, fh=fh:
                                        E.matmul(dpair[:, fh * 512:fh * 512 + 512],
                                                 h_e[:, ii * SEQ + tt * 128: ii * SEQ + (tt + 1) * 128],
                                                 wd_sb[:, ii * 1024 + fh * 512: ii * 1024 + fh * 512 + 512],
                                                 start=(ii == 0), stop=(ii == NI - 1)),
                                        inc=(ii == NI - 1 and fh == 1))
                            wd_freeT = n_dmm
                            n_last_dn_mm = n_dmm
                            ttg = b * 16 + tt
                            scale_ap = c_one[:] if e == 8 else cw_all[:, ttg * 8 + e: ttg * 8 + e + 1]
                            p.w("se", "te", n_dmm)
                            if e == 8:
                                n_dn = p.se(lambda E, dpair=dpair, tt=tt:
                                    E.activation(y_acc[:, tt * 1024: (tt + 1) * 1024],
                                                 dpair[:], AF.Copy))
                                pair_free[ucount % 3] = ("se", n_dn)
                                n_se_shared = n_dn
                            else:
                                dtb = dn_tmp[tt % 2]
                                p.w("se", "ve", dn_free[tt % 2])
                                n_dn = p.se(lambda E, dpair=dpair, o=dtb, s=scale_ap:
                                    E.activation(o, dpair[:], AF.Copy, scale=s))
                                pair_free[ucount % 3] = ("se", n_dn)
                                p.w("ve", "se", n_dn)
                                if first_v_add_waits == 0:
                                    p.w("ve", "se", n_se_shared)
                                    first_v_add_waits = 1
                                dn_free[tt % 2] = p.ve(lambda E, tt=tt, dtb=dtb:
                                    E.tensor_tensor(y_acc[:, tt * 1024: (tt + 1) * 1024],
                                                    y_acc[:, tt * 1024: (tt + 1) * 1024],
                                                    dtb, ALU.add))
                            ucount += 1
                n_vdone = p.c["ve"]
                p.w("se", "ve", n_vdone)
                for tt in range(16):
                    gb = gbf[tt % 2]
                    p.w("se", "s_gb%d" % (tt % 2), gb_free[tt % 2])
                    n_gb = p.se(lambda E, gb=gb, tt=tt:
                        E.activation(gb, y_acc[:, tt * 1024:(tt + 1) * 1024], AF.Silu))
                    p.w("se", "se", n_gb)
                    gb_free[tt % 2] = p.dma("qa", "s_gb%d" % (tt % 2), lambda E, gb=gb, b=b, tt=tt:
                        E.dma_start(out=g_sp[b, tt], in_=gb))

            if debug:
                pass
                p.dma("qs", "s_ld", lambda E: E.dma_start(out=dbg["g"], in_=g_sp))

        if "p2" in phases:
            barrier()
            w_in_sb = bf16v(arA, 0, 8 * 4096)
            xn_b = bf16v(arC, 0, 8 * SEQ)
            cpad = [f32v(arB, 0, 2052), f32v(arB, 8208, 2052)]
            xsa = [f32v(arB, 16416, 2048), f32v(arB, 24608, 2048)]
            rev = [f32v(arB, 32800, 2048), f32v(arB, 40992, 2048)]
            xsf_st = [f32v(arD, 0, 2048), f32v(arD, 8192, 2048)]
            xsbf_st = [bf16v(arD, 16384, 2048), bf16v(arD, 20480, 2048)]

            WSEMS = ("s_w", "s_w1", "s_w2", "s_w3")
            for k in range(8):
                n_win = p.dma("qp", WSEMS[k % 4], lambda E, k=k: E.dma_start(
                    out=w_in_sb[:, k * 4096:(k + 1) * 4096], in_=w_in[k * 128:(k + 1) * 128, :]))
            for ws in WSEMS:
                p.w("te", ws, p.c[ws])
            for i in range(2):
                p.ve(lambda E, i=i: E.memset(cpad[i][:, 0:3], 0.0))
            n_padz = p.c["ve"]
            p.w("se", "ve", n_padz)

            pair_free = [("ve", 0), ("ve", 0), ("ve", 0)]
            cpad_free = [0, 0]
            xsa_free = [0, 0]
            xsf_free = [0, 0]
            xsfV_free = [0, 0]
            xsbf_free = [0, 0]
            rev_free = [0, 0]
            ucount = 0
            for b in range(NB):
                p.w("sy", "te", p.c["te"])
                n_xnb = p.dma("qs", "s_ld", lambda E, b=b: E.dma_start(
                    out=xn_b.rearrange("p (d t) -> p d t", d=8), in_=xnT_sp[b]))
                p.w("te", "s_ld", n_xnb)
                for m in range(32):
                    buf = ucount % 2
                    pA = PAIR[(ucount * 2) % 3]
                    pB = PAIR[(ucount * 2 + 1) % 3]
                    for pi, pp in enumerate((pA, pB)):
                        eng, val = pair_free[(ucount * 2 + pi) % 3]
                        p.w("te", eng, val)
                        for k in range(8):
                            for sl in range(2):
                                n_mm = p.te(lambda E, pp=pp, k=k, m=m, pi=pi, sl=sl:
                                    E.matmul(pp[:, sl * 512:(sl + 1) * 512],
                                             w_in_sb[:, k * 4096 + m * 128: k * 4096 + (m + 1) * 128],
                                             xn_b[:, k * SEQ + pi * 1024 + sl * 512: k * SEQ + pi * 1024 + (sl + 1) * 512],
                                             start=(k == 0), stop=(k == 7)),
                                    inc=(k == 7 and sl == 1))
                    p.w("se", "te", n_mm)
                    if m < 16:
                        p.w("se", "ve", cpad_free[buf])
                        for pi, pp in enumerate((pA, pB)):
                            n_ev = p.se(lambda E, pp=pp, o=cpad[buf], pi=pi, m=m:
                                E.activation(o[:, 3 + pi * 1024: 3 + (pi + 1) * 1024], pp[:],
                                             AF.Silu, bias=biases[:, m:m + 1]))
                        pair_free[(ucount * 2) % 3] = ("se", n_ev)
                        pair_free[(ucount * 2 + 1) % 3] = ("se", n_ev)
                        p.w("ve", "se", n_ev)
                        p.w("ve", "se", xsa_free[buf])
                        p.ve(lambda E, o=xsa[buf], i=cpad[buf], m=m:
                            E.tensor_scalar(o, i[:, 3:3 + SEQ], convw_sb[:, m * 4 + 3: m * 4 + 4], None, ALU.mult))
                        for dd in range(1, 4):
                            n_cv = p.ve(lambda E, o=xsa[buf], i=cpad[buf], m=m, dd=dd:
                                E.scalar_tensor_tensor(o, i[:, 3 - dd: 3 - dd + SEQ],
                                                       convw_sb[:, m * 4 + 3 - dd: m * 4 + 4 - dd],
                                                       o, ALU.mult, ALU.add))
                        cpad_free[buf] = n_cv
                        p.w("se", "ve", n_cv)
                        p.w("se", "s_xsf%d" % buf, xsf_free[buf])
                        p.w("se", "ve", xsfV_free[buf])
                        n_xs = p.se(lambda E, o=xsf_st[buf], i=xsa[buf], m=m:
                            E.activation(o, i, AF.Silu, bias=convb_sb[:, m:m + 1]))
                        xsa_free[buf] = n_xs
                        p.w("ve", "se", n_xs)
                        p.w("ve", "s_xsb%d" % buf, xsbf_free[buf])
                        n_xbf = p.ve(lambda E, o=xsbf_st[buf], i=xsf_st[buf]: E.tensor_copy(o, i))
                        xsfV_free[buf] = n_xbf
                        p.w("sy", "se", n_xs)
                        xsf_free[buf] = p.dma("qs", "s_xsf%d" % buf, lambda E, i=xsf_st[buf], b=b, m=m:
                            E.dma_start(out=xs_sp[b, m], in_=i))
                        p.w("se", "ve", n_xbf)
                        xsbf_free[buf] = p.dma("qa", "s_xsb%d" % buf, lambda E, i=xsbf_st[buf], b=b, m=m:
                            E.dma_start(out=xsbf_sp[b, :, m, :], in_=i))
                    else:
                        p.w("se", "s_rv%d" % buf, rev_free[buf])
                        for pi, pp in enumerate((pA, pB)):
                            n_ev = p.se(lambda E, pp=pp, o=rev[buf], pi=pi, m=m:
                                E.activation(o[:, pi * 1024:(pi + 1) * 1024], pp[:],
                                             AF.Silu, bias=biases[:, m:m + 1]))
                        pair_free[(ucount * 2) % 3] = ("se", n_ev)
                        pair_free[(ucount * 2 + 1) % 3] = ("se", n_ev)
                        p.w("sy", "se", n_ev)
                        rev_free[buf] = p.dma("qs", "s_rv%d" % buf, lambda E, i=rev[buf], b=b, m=m:
                            E.dma_start(out=res_sp[b, m - 16], in_=i))
                    ucount += 1

            if debug:
                pass
                pass
                p.dma("qs", "s_ld", lambda E: E.dma_start(out=dbg["xs"], in_=xs_sp))
                p.dma("qs", "s_ld", lambda E: E.dma_start(out=dbg["res"], in_=res_sp))

        if "p3a" in phases:
            barrier()
            w_dt_sb = bf16v(arA, 0, 16 * SEQ)
            xsbf_b = bf16v(arB, 0, 16 * SEQ)
            tA2 = [f32v(arC, 0, 2048), f32v(arC, 8192, 2048)]
            tB2 = [f32v(arC, 16384, 2048), f32v(arC, 24576, 2048)]
            tC2 = [f32v(arD, 0, 2048), f32v(arD, 8192, 2048)]
            xsf = [f32v(arD, 16384, 2048), f32v(arD, 16384, 2048)]
            resf_b = f32v(arD, 24576, 2048)
            ybf_st = [bf16v(arD, 32768, 2048), bf16v(arD, 32768, 2048)]

            WSEMS = ("s_w", "s_w1", "s_w2", "s_w3")
            for k in range(16):
                n_wdt = p.dma("qp", WSEMS[k % 4], lambda E, k=k: E.dma_start(
                    out=w_dt_sb[:, k * SEQ:(k + 1) * SEQ], in_=w_dt[k * 128:(k + 1) * 128, :]))
            for ws in WSEMS:
                p.w("te", ws, p.c[ws])

            pair_free = [("ve", 0), ("ve", 0), ("ve", 0)]
            xsf_load = 0; xsf_freeV = 0
            resf_load = 0; resf_freeV = 0
            ybf_free = 0
            t_free = [0, 0]
            ucount = 0
            for b in range(NB):
                p.w("sy", "te", p.c["te"])
                n_xsb = p.dma("qs", "s_ld", lambda E, b=b: E.dma_start(
                    out=xsbf_b.rearrange("p (m t) -> p m t", m=16), in_=xsbf_sp[b]))
                p.w("te", "s_ld", n_xsb)
                for m in range(16):
                    buf = ucount % 2
                    tA = tA2[buf]; tB = tB2[buf]; tC = tC2[buf]
                    p.w("sy", "ve", xsf_freeV)
                    xsf_load = p.dma("qs", "s_xf0", lambda E, b=b, m=m:
                        E.dma_start(out=xsf[0], in_=xs_sp[b, m]))
                    p.w("se", "ve", resf_freeV)
                    resf_load = p.dma("qa", "s_rs", lambda E, b=b, m=m: E.dma_start(out=resf_b, in_=res_sp[b, m]))
                    pA = PAIR[(ucount * 2) % 3]
                    pB = PAIR[(ucount * 2 + 1) % 3]
                    for pi, pp in enumerate((pA, pB)):
                        eng, val = pair_free[(ucount * 2 + pi) % 3]
                        p.w("te", eng, val)
                        for k in range(16):
                            for sl in range(2):
                                n_mm = p.te(lambda E, pp=pp, k=k, m=m, pi=pi, sl=sl:
                                    E.matmul(pp[:, sl * 512:(sl + 1) * 512],
                                             w_dt_sb[:, k * SEQ + m * 128: k * SEQ + (m + 1) * 128],
                                             xsbf_b[:, k * SEQ + pi * 1024 + sl * 512: k * SEQ + pi * 1024 + (sl + 1) * 512],
                                             start=(k == 0), stop=(k == 15)),
                                    inc=(k == 15 and sl == 1))
                    p.w("se", "te", n_mm)
                    p.w("se", "ve", t_free[buf])
                    for pi, pp in enumerate((pA, pB)):
                        n_e1 = p.se(lambda E, pp=pp, pi=pi, m=m, tA=tA:
                            E.activation(tA[:, pi * 1024:(pi + 1) * 1024], pp[:], AF.Exp, bias=bdt_sb[:, m:m + 1]))
                    pair_free[(ucount * 2) % 3] = ("se", n_e1)
                    pair_free[(ucount * 2 + 1) % 3] = ("se", n_e1)
                    p.se(lambda E, tA=tA, tB=tB: E.activation(tB, tA, AF.Ln, bias=c_two[:]))
                    n_a = p.se(lambda E, tA=tA, tB=tB: E.activation(tA, tB, AF.Exp, bias=c_zero[:], scale=-1.0))
                    p.w("ve", "se", n_a)
                    p.w("ve", "s_xf0", xsf_load)
                    p.ve(lambda E, tA=tA, tB=tB: E.tensor_tensor(tB, tA, xsf[0], ALU.mult))
                    p.ve(lambda E, tB=tB: E.tensor_tensor(tB, xsf[0], tB, ALU.subtract))
                    p.ve(lambda E, tA=tA: E.memset(tA[:, 0:1], 0.0))
                    p.ve(lambda E, tB=tB: E.tensor_copy(tB[:, 0:1], xsf[0][:, 0:1]))
                    n_scan = p.ve(lambda E, tA=tA, tB=tB, tC=tC: E.tensor_tensor_scan(tC, tA, tB, c_zero[:], ALU.mult, ALU.add))
                    n_sk = p.ve(lambda E, m=m, tC=tC:
                        E.scalar_tensor_tensor(tC, xsf[0], dsk_sb[:, m:m + 1], tC, ALU.mult, ALU.add))
                    xsf_freeV = n_sk
                    p.w("ve", "s_rs", resf_load)
                    p.w("ve", "s_ybf0", ybf_free)
                    n_yb = p.ve(lambda E, tC=tC: E.tensor_tensor(ybf_st[0], tC, resf_b, ALU.mult))
                    resf_freeV = n_yb
                    t_free[buf] = n_yb
                    p.w("sy", "ve", n_yb)
                    ybf_free = p.dma("qs", "s_ybf0", lambda E, b=b, m=m:
                        E.dma_start(out=ybf_sp[b, :, m, :], in_=ybf_st[0]))
                    ucount += 1

            if debug:
                pass
                p.dma("qs", "s_ld", lambda E: E.dma_start(out=dbg["ybf"], in_=ybf_sp))

        if "p3b" in phases:
            barrier()
            y_b = bf16v(arA, 0, 16 * SEQ)
            w_out_sb = bf16v(arB, 0, 16 * 1024)
            gt = [bf16v(arB, 32768, 1024), bf16v(arB, 34816, 1024)]
            bout_sb = f32v(arB, 36864, 1024)
            lng_sb = f32v(arB, 40960, 1024)
            lnb_sb = f32v(arB, 45056, 1024)
            xt2 = [f32v(arC, 0, 1024), f32v(arC, 4096, 1024)]
            hsb = f32v(arC, 8192, 1024)
            xnt = f32v(arC, 12288, 1024)
            tE = f32v(arC, 16384, 1024)
            ost = [f32v(arC, 20480, 1024), f32v(arC, 24576, 1024)]
            bnst2 = f32v(arC, 28672, 12)
            bnag2 = f32v(arC, 28736, 2)
            rstd2 = f32v(arC, 28744, 1)
            tln2 = f32v(arC, 28748, 1)

            WSEMS = ("s_w", "s_w1", "s_w2", "s_w3")
            for k in range(16):
                n_wo = p.dma("qp", WSEMS[k % 4], lambda E, k=k: E.dma_start(
                    out=w_out_sb[:, k * 1024:(k + 1) * 1024], in_=w_out[k * 128:(k + 1) * 128, :]))
            for ws in WSEMS:
                p.w("te", ws, p.c[ws])
            for dst2, src2 in ((bout_sb, b_out_b), (lng_sb, ln_g_b), (lnb_sb, ln_b_b)):
                n_c3 = p.dma("qs", "s_ld", lambda E, d=dst2, s=src2: E.dma_start(out=d, in_=s))
            p.w("ve", "s_ld", n_c3)

            pair_free = [("ve", 0), ("ve", 0), ("ve", 0)]
            x_load = [0, 0]; x_freeV = [0, 0]
            g_load = [0, 0]; g_freeV = [0, 0]
            o_free = [0, 0]
            ucount = 0
            for b in range(NB):
                p.w("sy", "te", p.c["te"])
                n_yb = p.dma("qs", "s_ld", lambda E, b=b: E.dma_start(
                    out=y_b.rearrange("p (m t) -> p m t", m=16), in_=ybf_sp[b]))
                p.w("te", "s_ld", n_yb)
                for tt in range(16):
                    buf = ucount % 2
                    p.w("sy", "ve", x_freeV[buf])
                    x_load[buf] = p.dma("qs", "s_x%d" % buf, lambda E, b=b, tt=tt, d=xt2[buf]:
                        E.dma_start(out=d, in_=x[b, tt * 128:(tt + 1) * 128, :]))
                    p.w("se", "ve", g_freeV[buf])
                    g_load[buf] = p.dma("qa", "s_g%d" % buf, lambda E, b=b, tt=tt, d=gt[buf]:
                        E.dma_start(out=d, in_=g_sp[b, tt]))
                    dpair = PAIR[ucount % 3]
                    eng, val = pair_free[ucount % 3]
                    p.w("te", eng, val)
                    for k in range(16):
                        for sl in range(2):
                            n_mm = p.te(lambda E, dpair=dpair, k=k, tt=tt, sl=sl:
                                E.matmul(dpair[:, sl * 512:(sl + 1) * 512],
                                         y_b[:, k * SEQ + tt * 128: k * SEQ + (tt + 1) * 128],
                                         w_out_sb[:, k * 1024 + sl * 512: k * 1024 + (sl + 1) * 512],
                                         start=(k == 0), stop=(k == 15)),
                                inc=(k == 15 and sl == 1))
                    p.w("ve", "s_x%d" % buf, x_load[buf])
                    ttg = b * 16 + tt
                    p.ve(lambda E, d=xt2[buf], ttg=ttg:
                        E.tensor_scalar(xnt, d, ln1s[:, ttg * 2:ttg * 2 + 1], ln1s[:, ttg * 2 + 1:ttg * 2 + 2],
                                        ALU.subtract, ALU.mult))
                    p.w("ve", "te", n_mm)
                    n_h1 = p.ve(lambda E, dpair=dpair: E.tensor_tensor(hsb, dpair[:], xnt, ALU.add))
                    pair_free[ucount % 3] = ("ve", n_h1)
                    p.ve(lambda E: E.tensor_tensor(hsb, hsb, bout_sb, ALU.add))
                    p.ve(lambda E: E.bn_stats(bnst2[:, 0:6], hsb[:, 0:512]))
                    p.ve(lambda E: E.bn_stats(bnst2[:, 6:12], hsb[:, 512:1024]))
                    n_ag2 = p.ve(lambda E: E.bn_aggr(bnag2, bnst2))
                    p.w("se", "ve", n_ag2)
                    p.se(lambda E: E.activation(tln2, bnag2[:, 1:2], AF.Ln, bias=c_eps[:]))
                    n_rs2 = p.se(lambda E: E.activation(rstd2, tln2, AF.Exp, bias=c_zero[:], scale=-0.5))
                    p.w("ve", "se", n_rs2)
                    p.ve(lambda E: E.tensor_scalar(tE, hsb, bnag2[:, 0:1], rstd2, ALU.subtract, ALU.mult))
                    p.ve(lambda E: E.tensor_tensor(tE, tE, lng_sb, ALU.mult))
                    p.ve(lambda E: E.tensor_tensor(tE, tE, lnb_sb, ALU.add))
                    p.w("ve", "s_g%d" % buf, g_load[buf])
                    n_sg2 = p.ve(lambda E, d=gt[buf]: E.tensor_tensor(tE, tE, d, ALU.mult))
                    g_freeV[buf] = n_sg2
                    p.w("ve", "s_o%d" % buf, o_free[buf])
                    n_out = p.ve(lambda E, d=xt2[buf], o=ost[buf]: E.tensor_tensor(o, tE, d, ALU.add))
                    x_freeV[buf] = n_out
                    p.w("sy", "ve", n_out)
                    o_free[buf] = p.dma("qs", "s_o%d" % buf, lambda E, o=ost[buf], b=b, tt=tt:
                        E.dma_start(out=out[b, tt * 128:(tt + 1) * 128, :], in_=o))
                    ucount += 1

        for kk, vv in p.c.items():
            if kk not in ("te", "ve", "se") and vv > 0:
                p.w("sy", kk, vv)

        with nc.Block() as blk:
            @blk.sync
            def _(E):
                for f in p.ops["sy"]:
                    f(E)

            @blk.tensor
            def _(E):
                for f in p.ops["te"]:
                    f(E)

            @blk.vector
            def _(E):
                for f in p.ops["ve"]:
                    f(E)

            @blk.scalar
            def _(E):
                for f in p.ops["se"]:
                    f(E)

            @blk.gpsimd
            def _(E):
                for f in p.ops["po"]:
                    f(E)

    return nc


def make_aux(inputs):
    aux = {}
    aux["b_in2"] = np.ascontiguousarray(inputs["L0_attn_b_in"].reshape(32, 128).T)
    cw4 = inputs["L0_attn_conv_w"][:, 0, :].reshape(16, 128, 4)
    aux["conv_w2"] = np.ascontiguousarray(np.transpose(cw4, (1, 0, 2)).reshape(128, 64))
    aux["conv_b2"] = np.ascontiguousarray(inputs["L0_attn_conv_b"].reshape(16, 128).T)
    aux["b_dt2"] = np.ascontiguousarray(inputs["L0_attn_b_dt"].reshape(16, 128).T)
    aux["dsk2"] = np.ascontiguousarray(inputs["L0_attn_D_skip"].reshape(16, 128).T)
    aux["b_out_b"] = np.ascontiguousarray(np.broadcast_to(inputs["L0_attn_b_out"], (128, 1024)))
    aux["ln_g_b"] = np.ascontiguousarray(np.broadcast_to(inputs["L0_attn_ln_g"], (128, 1024)))
    aux["ln_b_b"] = np.ascontiguousarray(np.broadcast_to(inputs["L0_attn_ln_b"], (128, 1024)))
    wrT = np.ascontiguousarray(inputs["L0_ffn_w_router"]).T.reshape(8, 128, 8)
    aux["wr2"] = np.ascontiguousarray(np.transpose(wrT, (1, 0, 2)).reshape(128, 64))
    aux["ident"] = np.eye(128, dtype=ml_dtypes.bfloat16)
    return aux


def make_in_maps(inputs):
    aux = make_aux(inputs)
    in_maps = []
    for c in range(8):
        m = {
            "x": np.ascontiguousarray(inputs["L0_attn_x"][c * 2:(c + 1) * 2]),
            "w_in": inputs["L0_attn_w_in"],
            "w_dt": inputs["L0_attn_w_dt"],
            "w_out": inputs["L0_attn_w_out"],
            "wg": inputs["L0_ffn_wg"], "wu": inputs["L0_ffn_wu"], "wd": inputs["L0_ffn_wd"],
            "wg_s": inputs["L0_ffn_wg_s"], "wu_s": inputs["L0_ffn_wu_s"], "wd_s": inputs["L0_ffn_wd_s"],
        }
        m.update(aux)
        in_maps.append(m)
    return in_maps


def kernel(**inputs):
    nc = build(debug=False, phases=PHASES)
    in_maps = make_in_maps(inputs)
    res = run_bass_kernel_spmd(nc, in_maps, core_ids=list(range(8)))
    return np.concatenate([r["out"] for r in res.results], axis=0)


# version 05
# speedup vs baseline: 1.0904x; 1.0904x over previous
import numpy as np
import ml_dtypes
import concourse.bass as bass
import concourse.mybir as mybir
from concourse.bass_utils import run_bass_kernel_spmd
from contextlib import ExitStack

F32 = mybir.dt.float32
BF16 = mybir.dt.bfloat16
AF = mybir.ActivationFunctionType
ALU = mybir.AluOpType
AX = mybir.AxisListType

D = 1024
DI = 2048
SEQ = 2048
NB = 2
NTT = 16
NE = 8
IE = 1408
NI = 11
EPS = 1e-5

PHASES = ("p0", "p1", "p2", "p3a", "p3b")
DEBUG = False


class Prog:
    ENGS = ("te", "ve", "se", "po", "sy")

    def __init__(self, nc, sems):
        self.nc = nc
        self.sem = sems
        self.ops = {e: [] for e in self.ENGS}
        self.c = {k: 0 for k in sems}

    def w(self, eng, sem, val):
        if val > 0:
            self.ops[eng].append(lambda E, s=self.sem[sem], v=val: E.wait_ge(s, v))

    def te(self, fn, inc=True):
        if inc:
            self.c["te"] += 1
            n = self.c["te"]
            self.ops["te"].append(lambda E, f=fn, s=self.sem["te"]: f(E).then_inc(s, 1))
            return n
        self.ops["te"].append(fn)
        return self.c["te"]

    def _serial(self, eng, fn):
        prev = self.c[eng]
        if prev > 0:
            self.ops[eng].append(lambda E, s=self.sem[eng], v=prev: E.wait_ge(s, v))
        self.c[eng] += 1
        n = self.c[eng]
        self.ops[eng].append(lambda E, f=fn, s=self.sem[eng]: f(E).then_inc(s, 1))
        return n

    def ve(self, fn):
        return self._serial("ve", fn)

    def po(self, fn):
        return self._serial("po", fn)

    def se(self, fn):
        return self._serial("se", fn)

    def dma(self, q, sem, fn):
        eng = {"qs": "sy", "qa": "se", "qp": "po"}[q]
        prev = self.c[sem]
        if prev > 0:
            self.ops[eng].append(lambda E, s=self.sem[sem], v=prev: E.wait_ge(s, v))
        self.c[sem] += 16
        n = self.c[sem]
        self.ops[eng].append(lambda E, f=fn, s=self.sem[sem]: f(E).then_inc(s, 16))
        return n


def build(debug=DEBUG, phases=PHASES):
    nc = bass.Bass("TRN2", target_bir_lowering=False)

    x = nc.dram_tensor("x", [NB, SEQ, D], F32, kind="ExternalInput").ap()
    w_in = nc.dram_tensor("w_in", [D, 2 * DI], F32, kind="ExternalInput").ap()
    b_in2 = nc.dram_tensor("b_in2", [128, 32], F32, kind="ExternalInput").ap()
    conv_w2 = nc.dram_tensor("conv_w2", [128, 64], F32, kind="ExternalInput").ap()
    conv_b2 = nc.dram_tensor("conv_b2", [128, 16], F32, kind="ExternalInput").ap()
    w_dt = nc.dram_tensor("w_dt", [DI, DI], F32, kind="ExternalInput").ap()
    b_dt2 = nc.dram_tensor("b_dt2", [128, 16], F32, kind="ExternalInput").ap()
    dsk2 = nc.dram_tensor("dsk2", [128, 16], F32, kind="ExternalInput").ap()
    w_out = nc.dram_tensor("w_out", [DI, D], F32, kind="ExternalInput").ap()
    b_out_b = nc.dram_tensor("b_out_b", [128, D], F32, kind="ExternalInput").ap()
    ln_g_b = nc.dram_tensor("ln_g_b", [128, D], F32, kind="ExternalInput").ap()
    ln_b_b = nc.dram_tensor("ln_b_b", [128, D], F32, kind="ExternalInput").ap()
    wr2 = nc.dram_tensor("wr2", [128, 64], F32, kind="ExternalInput").ap()
    wg = nc.dram_tensor("wg", [NE, D, IE], F32, kind="ExternalInput").ap()
    wu = nc.dram_tensor("wu", [NE, D, IE], F32, kind="ExternalInput").ap()
    wd = nc.dram_tensor("wd", [NE, IE, D], F32, kind="ExternalInput").ap()
    wg_s = nc.dram_tensor("wg_s", [D, IE], F32, kind="ExternalInput").ap()
    wu_s = nc.dram_tensor("wu_s", [D, IE], F32, kind="ExternalInput").ap()
    wd_s = nc.dram_tensor("wd_s", [IE, D], F32, kind="ExternalInput").ap()
    ident = nc.dram_tensor("ident", [128, 128], BF16, kind="ExternalInput").ap()

    out = nc.dram_tensor("out", [NB, SEQ, D], F32, kind="ExternalOutput").ap()

    xnT_sp = nc.dram_tensor("xnT_sp", [NB, 128, 8, SEQ], BF16, kind="Internal").ap()
    g_sp = nc.dram_tensor("g_sp", [NB, NTT, 128, D], BF16, kind="Internal").ap()
    xs_sp = nc.dram_tensor("xs_sp", [NB, 16, 128, SEQ], F32, kind="Internal").ap()
    xsbf_sp = nc.dram_tensor("xsbf_sp", [NB, 128, 16, SEQ], BF16, kind="Internal").ap()
    res_sp = nc.dram_tensor("res_sp", [NB, 16, 128, SEQ], F32, kind="Internal").ap()
    ybf_sp = nc.dram_tensor("ybf_sp", [NB, 128, 16, SEQ], BF16, kind="Internal").ap()

    dbg = {}
    if debug:
        def dbg_t(name, shape, dt):
            dbg[name] = nc.dram_tensor("dbg_" + name, shape, dt, kind="ExternalOutput").ap()
        dbg_t("xnT", [NB, 128, 8, SEQ], BF16)
        dbg_t("cw", [128, 256], F32)
        dbg_t("g", [NB, NTT, 128, D], BF16)
        dbg_t("xs", [NB, 16, 128, SEQ], F32)
        dbg_t("res", [NB, 16, 128, SEQ], F32)
        dbg_t("ybf", [NB, 128, 16, SEQ], BF16)

    with ExitStack() as stk:
        sb = lambda name, shape, dt=F32: stk.enter_context(nc.sbuf_tensor(name, shape, dt))
        sems = {}
        DMA_SEMS = ("s_ld", "s_w", "s_w1", "s_w2", "s_w3", "s_x0", "s_x1", "s_sp0", "s_sp1",
                    "s_wg0", "s_wg1", "s_wg2", "s_wg3", "s_wd", "s_gb0", "s_gb1",
                    "s_xsf0", "s_xsf1", "s_xsb0", "s_xsb1", "s_rv0", "s_rv1",
                    "s_xf0", "s_xf1", "s_rs", "s_ybf0", "s_ybf1",
                    "s_g0", "s_g1", "s_o0", "s_o1")
        for s in ("te", "ve", "se", "po") + DMA_SEMS:
            sems[s] = stk.enter_context(nc.semaphore("sem_" + s))
        p = Prog(nc, sems)

        pf = stk.enter_context(nc.psum_tensor("pf", [128, 3072], F32))
        pbf = [stk.enter_context(nc.psum_tensor(f"pbf{i}", [128, 1024], BF16)) for i in range(2)]
        PAIR = [pf[:, k * 1024:(k + 1) * 1024] for k in range(3)]

        arA = sb("arA", [128, 16384])
        arB = sb("arB", [128, 16384])
        arC = sb("arC", [128, 8192])
        arD = sb("arD", [128, 9728])

        def f32v(ar, off, n):
            assert off % 4 == 0
            return ar[:, off // 4: off // 4 + n]

        def bf16v(ar, off, n):
            assert off % 2 == 0
            return ar[:].bitcast(BF16)[:, off // 2: off // 2 + n]

        ident_sb = sb("ident_sb", [128, 128], BF16)
        wr_sb = sb("wr_sb", [128, 64], BF16)
        biases = sb("biases", [128, 32])
        convw_sb = sb("convw_sb", [128, 64])
        convb_sb = sb("convb_sb", [128, 16])
        bdt_sb = sb("bdt_sb", [128, 16])
        dsk_sb = sb("dsk_sb", [128, 16])
        cw_all = sb("cw_all", [128, 32 * 8])
        ln1s = sb("ln1s", [128, 64])
        c_eps = sb("c_eps", [128, 1])
        c_two = sb("c_two", [128, 1])
        c_one = sb("c_one", [128, 1])
        c_zero = sb("c_zero", [128, 1])
        ones8 = sb("ones8", [128, 8])

        for dst, src in ((ident_sb, ident), (biases, b_in2), (convw_sb, conv_w2),
                         (convb_sb, conv_b2), (bdt_sb, b_dt2), (dsk_sb, dsk2)):
            p.dma("qs", "s_ld", lambda E, d=dst, s=src: E.dma_start(out=d[:], in_=s))
        n_const = p.c["s_ld"]
        n_wr = p.dma("qp", "s_w", lambda E: E.dma_start(out=wr_sb[:], in_=wr2))
        for buf, v in ((c_eps, EPS), (c_two, 2.0), (c_one, 1.0), (c_zero, 0.0), (ones8, 1.0)):
            p.ve(lambda E, b=buf, v=v: E.memset(b[:], v))

        def barrier():
            snap = {kk: vv for kk, vv in p.c.items() if vv > 0}
            for eng in ("te", "ve", "se", "sy", "po"):
                for ss, vv in snap.items():
                    p.w(eng, ss, vv)

        if "p0" in phases:
            xtile = [f32v(arD, 0, 1024), f32v(arD, 4096, 1024)]
            xnbf = [bf16v(arD, 8192, 1024), bf16v(arD, 10240, 1024)]
            stage = [bf16v(arD, 12288, 1024), bf16v(arD, 14336, 1024)]
            bnst = f32v(arD, 16384, 12)
            bnag = f32v(arD, 16448, 2)
            rstd0 = f32v(arD, 16456, 1)
            tln = f32v(arD, 16460, 1)
            off = 16464
            smalls = {}
            for nm in ("sc8", "e8", "eq1", "cum8", "ile8", "msk1", "sv2", "eq2", "mskT", "mskd"):
                smalls[nm] = f32v(arD, off, 8); off += 32
            for nm in ("m1", "nm1", "m2", "den8", "rden"):
                smalls[nm] = f32v(arD, off, 1); off += 4
            sc8 = smalls["sc8"]; e8 = smalls["e8"]; eq1 = smalls["eq1"]; cum8 = smalls["cum8"]
            ile8 = smalls["ile8"]; msk1 = smalls["msk1"]; sv2 = smalls["sv2"]; eq2 = smalls["eq2"]
            mskT = smalls["mskT"]; mskd = smalls["mskd"]; m1 = smalls["m1"]; nm1 = smalls["nm1"]
            m2 = smalls["m2"]; den8 = smalls["den8"]; rden = smalls["rden"]

            p.w("te", "s_ld", n_const)
            p.w("te", "s_w", n_wr)

            xload = [0, 0]
            xfree = [0, 0]
            stfree = [(0, 0), (0, 0)]
            ptfree = [0, 0]
            prfree = 0
            xnbf_free = [0, 0]

            for tt in range(32):
                b, tl = divmod(tt, NTT)
                buf = tt % 2
                p.w("sy", "ve", xfree[buf])
                xload[buf] = p.dma("qs", "s_x%d" % buf, lambda E, b=b, tl=tl, d=xtile[buf]:
                    E.dma_start(out=d, in_=x[b, tl * 128:(tl + 1) * 128, :]))
                p.w("ve", "s_x%d" % buf, xload[buf])
                p.ve(lambda E, d=xtile[buf]: E.bn_stats(bnst[:, 0:6], d[:, 0:512]))
                p.ve(lambda E, d=xtile[buf]: E.bn_stats(bnst[:, 6:12], d[:, 512:1024]))
                n_ag = p.ve(lambda E: E.bn_aggr(bnag, bnst))
                p.w("se", "ve", n_ag)
                p.se(lambda E: E.activation(tln, bnag[:, 1:2], AF.Ln, bias=c_eps[:], scale=1.0))
                n_rs = p.se(lambda E: E.activation(rstd0, tln, AF.Exp, bias=c_zero[:], scale=-0.5))
                p.w("ve", "se", n_rs)
                p.ve(lambda E, tt=tt: E.tensor_copy(ln1s[:, tt * 2:tt * 2 + 1], bnag[:, 0:1]))
                p.ve(lambda E, tt=tt: E.tensor_copy(ln1s[:, tt * 2 + 1:tt * 2 + 2], rstd0))
                p.w("ve", "te", xnbf_free[buf])
                n_xn = p.ve(lambda E, d=xtile[buf], o=xnbf[buf]:
                    E.tensor_scalar(o, d, bnag[:, 0:1], rstd0, ALU.subtract, ALU.mult))
                xfree[buf] = n_xn
                p.w("te", "ve", n_xn)
                p.w("te", "ve", ptfree[buf])
                for dd in range(8):
                    n_tr = p.te(lambda E, dd=dd, o=pbf[buf], i=xnbf[buf]:
                        E.transpose(o[:, dd * 128:(dd + 1) * 128], i[:, dd * 128:(dd + 1) * 128], ident_sb[:]),
                        inc=(dd == 7))
                xnbf_free[buf] = n_tr
                p.w("ve", "te", n_tr)
                p.w("ve", "s_sp%d" % buf, stfree[buf][0])
                p.w("ve", "te", stfree[buf][1])
                n_ev = p.ve(lambda E, o=stage[buf], i=pbf[buf]: E.tensor_copy(o, i[:]))
                ptfree[buf] = n_ev
                p.w("se", "ve", n_ev)
                n_sp = p.dma("qa", "s_sp%d" % buf, lambda E, b=b, tl=tl, i=stage[buf]:
                    E.dma_start(out=xnT_sp[b, :, :, tl * 128:(tl + 1) * 128],
                                in_=i.rearrange("p (d c) -> p d c", d=8)))
                p.w("te", "ve", n_ev)
                p.w("te", "ve", prfree)
                for dd in range(8):
                    n_rt = p.te(lambda E, dd=dd, i=stage[buf]:
                        E.matmul(PAIR[0][:, 0:8], i[:, dd * 128:(dd + 1) * 128], wr_sb[:, dd * 8:(dd + 1) * 8],
                                 start=(dd == 0), stop=(dd == 7)), inc=(dd == 7))
                stfree[buf] = (n_sp, n_rt)
                p.w("ve", "te", n_rt)
                n_cp = p.ve(lambda E: E.tensor_copy(sc8, PAIR[0][:, 0:8]))
                prfree = n_cp
                p.ve(lambda E: E.tensor_reduce(m1, sc8, AX.X, ALU.max))
                n_m1 = p.ve(lambda E: E.tensor_scalar(nm1, m1, -1.0, None, ALU.mult))
                p.w("se", "ve", n_m1)
                n_e8 = p.se(lambda E: E.activation(e8, sc8, AF.Exp, bias=nm1, scale=1.0))
                p.ve(lambda E: E.tensor_scalar(eq1, sc8, m1, None, ALU.is_ge))
                p.ve(lambda E: E.tensor_tensor_scan(cum8, ones8[:], eq1, c_zero[:], ALU.mult, ALU.add))
                p.ve(lambda E: E.tensor_scalar(ile8, cum8, 1.5, None, ALU.is_le))
                p.ve(lambda E: E.tensor_tensor(msk1, eq1, ile8, ALU.mult))
                p.ve(lambda E: E.scalar_tensor_tensor(sv2, msk1, -1e30, sc8, ALU.mult, ALU.add))
                p.ve(lambda E: E.tensor_reduce(m2, sv2, AX.X, ALU.max))
                p.ve(lambda E: E.tensor_scalar(eq2, sv2, m2, None, ALU.is_ge))
                p.ve(lambda E: E.tensor_tensor_scan(cum8, ones8[:], eq2, c_zero[:], ALU.mult, ALU.add))
                p.ve(lambda E: E.tensor_scalar(ile8, cum8, 1.5, None, ALU.is_le))
                p.ve(lambda E: E.tensor_tensor(mskT, eq2, ile8, ALU.mult))
                p.ve(lambda E: E.tensor_tensor(mskT, mskT, msk1, ALU.add))
                p.w("ve", "se", n_e8)
                p.ve(lambda E: E.tensor_tensor(mskd, mskT, e8, ALU.mult))
                p.ve(lambda E: E.tensor_reduce(den8, mskd, AX.X, ALU.add))
                p.ve(lambda E: E.reciprocal(rden, den8))
                p.ve(lambda E, tt=tt: E.tensor_scalar(cw_all[:, tt * 8:(tt + 1) * 8], mskd, rden, None, ALU.mult))

            if debug:
                pass
                p.w("sy", "ve", p.c["ve"])
                p.dma("qs", "s_ld", lambda E: E.dma_start(out=dbg["xnT"], in_=xnT_sp))
                p.dma("qs", "s_ld", lambda E: E.dma_start(out=dbg["cw"], in_=cw_all[:]))

        if "p1" in phases:
            barrier()
            y_acc = arA[:, :]
            h_e = bf16v(arB, 0, NI * SEQ)
            xn_b = bf16v(arC, 0, 8 * SEQ)
            wd_sb = bf16v(arD, 0, NI * 1024)
            wgu = bf16v(arD, 22528, 4 * 1024)
            sg_tmp = [f32v(arD, 30720, 1024), f32v(arD, 34816, 1024)]
            dn_tmp = [f32v(arB, 45056, 1024), f32v(arB, 49152, 1024)]
            gbf = [bf16v(arB, 53248, 1024), bf16v(arB, 55296, 1024)]

            E_ORDER = [8] + list(range(8))

            def wgu_src(e, i, which):
                if e == 8:
                    t = wg_s if which == 0 else wu_s
                    return t[:, i * 128:(i + 1) * 128].rearrange("(k p) m -> p k m", p=128)
                t = wg if which == 0 else wu
                return t[e][:, i * 128:(i + 1) * 128].rearrange("(k p) m -> p k m", p=128)

            def wd_src(e):
                t = wd_s if e == 8 else wd[e]
                return t.rearrange("(it p) f -> p it f", p=128)

            pair_free = [("ve", 0), ("ve", 0), ("ve", 0)]
            slot_load = [0, 0, 0, 0]
            slot_freeT = [0, 0, 0, 0]
            wd_load = 0
            wd_freeT = 0
            n_last_dn_mm = 0
            n_last_h_ev = 0
            n_se_shared = 0
            sg_free = [0, 0]
            dn_free = [0, 0]
            gb_free = [0, 0]
            units = [(e, i) for e in E_ORDER for i in range(NI)]
            NU = len(units)

            for b in range(NB):
                p.w("sy", "te", p.c["te"])
                n_xnb = p.dma("qs", "s_ld", lambda E, b=b: E.dma_start(
                    out=xn_b.rearrange("p (d t) -> p d t", d=8), in_=xnT_sp[b]))
                p.w("te", "s_ld", n_xnb)
                first_v_add_waits = 0

                def issue_wgu_dma(uidx):
                    e, i = units[uidx]
                    for which in (0, 1):
                        slot = (uidx * 2 + which) % 4
                        p.w("po", "te", slot_freeT[slot])
                        slot_load[slot] = p.dma("qp", "s_wg%d" % slot, lambda E, e=e, i=i, w=which, s=slot:
                            E.dma_start(out=wgu[:, s * 1024:(s + 1) * 1024].rearrange("p (k m) -> p k m", k=8),
                                        in_=wgu_src(e, i, w)))

                issue_wgu_dma(0)
                issue_wgu_dma(1)

                ucount = 0
                for uidx, (e, i) in enumerate(units):
                    for th in range(2):
                        gp = PAIR[(ucount * 2) % 3]
                        up = PAIR[(ucount * 2 + 1) % 3]
                        gslot = (uidx * 2) % 4
                        uslot = (uidx * 2 + 1) % 4
                        if th == 0:
                            p.w("te", "s_wg%d" % gslot, slot_load[gslot])
                        for pi, (pp, slot) in enumerate(((gp, gslot), (up, uslot))):
                            if th == 0 and pi == 1:
                                p.w("te", "s_wg%d" % uslot, slot_load[uslot])
                            eng, val = pair_free[(ucount * 2 + pi) % 3]
                            p.w("te", eng, val)
                            for k in range(8):
                                for sl in range(2):
                                    n_mm = p.te(lambda E, pp=pp, slot=slot, k=k, sl=sl, th=th:
                                        E.matmul(pp[:, sl * 512:(sl + 1) * 512],
                                                 wgu[:, slot * 1024 + k * 128: slot * 1024 + (k + 1) * 128],
                                                 xn_b[:, k * SEQ + th * 1024 + sl * 512: k * SEQ + th * 1024 + (sl + 1) * 512],
                                                 start=(k == 0), stop=(k == 7)),
                                        inc=(k == 7 and sl == 1))
                        if th == 1:
                            slot_freeT[gslot] = n_mm
                            slot_freeT[uslot] = n_mm
                        stb = sg_tmp[ucount % 2]
                        p.w("se", "te", n_mm)
                        p.w("se", "ve", sg_free[ucount % 2])
                        n_sg = p.se(lambda E, pp=gp, o=stb: E.activation(o, pp[:], AF.Silu))
                        p.w("ve", "se", n_sg)
                        p.w("ve", "te", n_last_dn_mm)
                        n_h = p.ve(lambda E, pp=up, o=h_e, i=i, th=th, stb=stb:
                            E.tensor_tensor(o[:, i * SEQ + th * 1024: i * SEQ + (th + 1) * 1024],
                                            stb, pp[:], ALU.mult))
                        pair_free[(ucount * 2) % 3] = ("se", n_sg)
                        pair_free[(ucount * 2 + 1) % 3] = ("ve", n_h)
                        sg_free[ucount % 2] = n_h
                        n_last_h_ev = n_h
                        ucount += 1
                    if uidx + 2 < NU:
                        issue_wgu_dma(uidx + 2)
                    if 4 <= i <= 7:
                        ch = i - 4
                        lo, hi = (3 * ch, 3 * ch + 3) if ch < 3 else (9, 11)
                        if ch == 0:
                            p.w("po", "te", wd_freeT)
                        wd_load = p.dma("qp", "s_wd", lambda E, e=e, lo=lo, hi=hi:
                            E.dma_start(out=wd_sb[:, lo * 1024:hi * 1024].rearrange("p (it f) -> p it f", f=1024),
                                        in_=wd_src(e)[:, lo:hi, :]))
                    if i == NI - 1:
                        p.w("te", "s_wd", wd_load)
                        p.w("te", "ve", n_last_h_ev)
                        for tt in range(16):
                            dpair = PAIR[ucount % 3]
                            eng, val = pair_free[ucount % 3]
                            p.w("te", eng, val)
                            for ii in range(NI):
                                for fh in range(2):
                                    n_dmm = p.te(lambda E, dpair=dpair, ii=ii, tt=tt, fh=fh:
                                        E.matmul(dpair[:, fh * 512:fh * 512 + 512],
                                                 h_e[:, ii * SEQ + tt * 128: ii * SEQ + (tt + 1) * 128],
                                                 wd_sb[:, ii * 1024 + fh * 512: ii * 1024 + fh * 512 + 512],
                                                 start=(ii == 0), stop=(ii == NI - 1)),
                                        inc=(ii == NI - 1 and fh == 1))
                            wd_freeT = n_dmm
                            n_last_dn_mm = n_dmm
                            ttg = b * 16 + tt
                            scale_ap = c_one[:] if e == 8 else cw_all[:, ttg * 8 + e: ttg * 8 + e + 1]
                            p.w("se", "te", n_dmm)
                            if e == 8:
                                n_dn = p.se(lambda E, dpair=dpair, tt=tt:
                                    E.activation(y_acc[:, tt * 1024: (tt + 1) * 1024],
                                                 dpair[:], AF.Copy))
                                pair_free[ucount % 3] = ("se", n_dn)
                                n_se_shared = n_dn
                            else:
                                dtb = dn_tmp[tt % 2]
                                p.w("se", "ve", dn_free[tt % 2])
                                n_dn = p.se(lambda E, dpair=dpair, o=dtb, s=scale_ap:
                                    E.activation(o, dpair[:], AF.Copy, scale=s))
                                pair_free[ucount % 3] = ("se", n_dn)
                                p.w("ve", "se", n_dn)
                                if first_v_add_waits == 0:
                                    p.w("ve", "se", n_se_shared)
                                    first_v_add_waits = 1
                                dn_free[tt % 2] = p.ve(lambda E, tt=tt, dtb=dtb:
                                    E.tensor_tensor(y_acc[:, tt * 1024: (tt + 1) * 1024],
                                                    y_acc[:, tt * 1024: (tt + 1) * 1024],
                                                    dtb, ALU.add))
                            ucount += 1
                n_vdone = p.c["ve"]
                p.w("se", "ve", n_vdone)
                for tt in range(16):
                    gb = gbf[tt % 2]
                    p.w("se", "s_gb%d" % (tt % 2), gb_free[tt % 2])
                    n_gb = p.se(lambda E, gb=gb, tt=tt:
                        E.activation(gb, y_acc[:, tt * 1024:(tt + 1) * 1024], AF.Silu))
                    p.w("se", "se", n_gb)
                    gb_free[tt % 2] = p.dma("qa", "s_gb%d" % (tt % 2), lambda E, gb=gb, b=b, tt=tt:
                        E.dma_start(out=g_sp[b, tt], in_=gb))

            if debug:
                pass
                p.dma("qs", "s_ld", lambda E: E.dma_start(out=dbg["g"], in_=g_sp))

        if "p2" in phases:
            barrier()
            w_in_sb = bf16v(arA, 0, 8 * 4096)
            xn_b = bf16v(arC, 0, 8 * SEQ)
            cpad = [f32v(arB, 0, 2052), f32v(arB, 8208, 2052)]
            xsa = [f32v(arB, 16416, 2048), f32v(arB, 24608, 2048)]
            rev = [f32v(arB, 32800, 2048), f32v(arB, 40992, 2048)]
            xsf_st = [f32v(arD, 0, 2048), f32v(arD, 8192, 2048)]
            xsbf_st = [bf16v(arD, 16384, 2048), bf16v(arD, 20480, 2048)]

            WSEMS = ("s_w", "s_w1", "s_w2", "s_w3")
            for k in range(8):
                n_win = p.dma("qp", WSEMS[k % 4], lambda E, k=k: E.dma_start(
                    out=w_in_sb[:, k * 4096:(k + 1) * 4096], in_=w_in[k * 128:(k + 1) * 128, :]))
            for ws in WSEMS:
                p.w("te", ws, p.c[ws])
            for i in range(2):
                p.ve(lambda E, i=i: E.memset(cpad[i][:, 0:3], 0.0))
            n_padz = p.c["ve"]
            p.w("se", "ve", n_padz)

            pair_free = [("ve", 0), ("ve", 0), ("ve", 0)]
            cpad_free = [0, 0]
            xsa_free = [0, 0]
            xsf_free = [0, 0]
            xsfV_free = [0, 0]
            xsbf_free = [0, 0]
            rev_free = [0, 0]
            ucount = 0
            for b in range(NB):
                p.w("sy", "te", p.c["te"])
                n_xnb = p.dma("qs", "s_ld", lambda E, b=b: E.dma_start(
                    out=xn_b.rearrange("p (d t) -> p d t", d=8), in_=xnT_sp[b]))
                p.w("te", "s_ld", n_xnb)
                for m in range(32):
                    buf = ucount % 2
                    pA = PAIR[(ucount * 2) % 3]
                    pB = PAIR[(ucount * 2 + 1) % 3]
                    for pi, pp in enumerate((pA, pB)):
                        eng, val = pair_free[(ucount * 2 + pi) % 3]
                        p.w("te", eng, val)
                        for k in range(8):
                            for sl in range(2):
                                n_mm = p.te(lambda E, pp=pp, k=k, m=m, pi=pi, sl=sl:
                                    E.matmul(pp[:, sl * 512:(sl + 1) * 512],
                                             w_in_sb[:, k * 4096 + m * 128: k * 4096 + (m + 1) * 128],
                                             xn_b[:, k * SEQ + pi * 1024 + sl * 512: k * SEQ + pi * 1024 + (sl + 1) * 512],
                                             start=(k == 0), stop=(k == 7)),
                                    inc=(k == 7 and sl == 1))
                    p.w("se", "te", n_mm)
                    if m < 16:
                        p.w("se", "ve", cpad_free[buf])
                        for pi, pp in enumerate((pA, pB)):
                            n_ev = p.se(lambda E, pp=pp, o=cpad[buf], pi=pi, m=m:
                                E.activation(o[:, 3 + pi * 1024: 3 + (pi + 1) * 1024], pp[:],
                                             AF.Silu, bias=biases[:, m:m + 1]))
                        pair_free[(ucount * 2) % 3] = ("se", n_ev)
                        pair_free[(ucount * 2 + 1) % 3] = ("se", n_ev)
                        p.w("ve", "se", n_ev)
                        p.w("ve", "se", xsa_free[buf])
                        p.ve(lambda E, o=xsa[buf], i=cpad[buf], m=m:
                            E.tensor_scalar(o, i[:, 3:3 + SEQ], convw_sb[:, m * 4 + 3: m * 4 + 4], None, ALU.mult))
                        for dd in range(1, 4):
                            n_cv = p.ve(lambda E, o=xsa[buf], i=cpad[buf], m=m, dd=dd:
                                E.scalar_tensor_tensor(o, i[:, 3 - dd: 3 - dd + SEQ],
                                                       convw_sb[:, m * 4 + 3 - dd: m * 4 + 4 - dd],
                                                       o, ALU.mult, ALU.add))
                        cpad_free[buf] = n_cv
                        p.w("se", "ve", n_cv)
                        p.w("se", "s_xsf%d" % buf, xsf_free[buf])
                        p.w("se", "po", xsfV_free[buf])
                        n_xs = p.se(lambda E, o=xsf_st[buf], i=xsa[buf], m=m:
                            E.activation(o, i, AF.Silu, bias=convb_sb[:, m:m + 1]))
                        xsa_free[buf] = n_xs
                        p.w("po", "se", n_xs)
                        p.w("po", "s_xsb%d" % buf, xsbf_free[buf])
                        n_xbf = p.po(lambda E, o=xsbf_st[buf], i=xsf_st[buf]: E.tensor_copy(o, i))
                        xsfV_free[buf] = n_xbf
                        p.w("sy", "se", n_xs)
                        xsf_free[buf] = p.dma("qs", "s_xsf%d" % buf, lambda E, i=xsf_st[buf], b=b, m=m:
                            E.dma_start(out=xs_sp[b, m], in_=i))
                        p.w("se", "po", n_xbf)
                        xsbf_free[buf] = p.dma("qa", "s_xsb%d" % buf, lambda E, i=xsbf_st[buf], b=b, m=m:
                            E.dma_start(out=xsbf_sp[b, :, m, :], in_=i))
                    else:
                        p.w("se", "s_rv%d" % buf, rev_free[buf])
                        for pi, pp in enumerate((pA, pB)):
                            n_ev = p.se(lambda E, pp=pp, o=rev[buf], pi=pi, m=m:
                                E.activation(o[:, pi * 1024:(pi + 1) * 1024], pp[:],
                                             AF.Silu, bias=biases[:, m:m + 1]))
                        pair_free[(ucount * 2) % 3] = ("se", n_ev)
                        pair_free[(ucount * 2 + 1) % 3] = ("se", n_ev)
                        p.w("sy", "se", n_ev)
                        rev_free[buf] = p.dma("qs", "s_rv%d" % buf, lambda E, i=rev[buf], b=b, m=m:
                            E.dma_start(out=res_sp[b, m - 16], in_=i))
                    ucount += 1

            if debug:
                pass
                pass
                p.dma("qs", "s_ld", lambda E: E.dma_start(out=dbg["xs"], in_=xs_sp))
                p.dma("qs", "s_ld", lambda E: E.dma_start(out=dbg["res"], in_=res_sp))

        if "p3a" in phases:
            barrier()
            w_dt_sb = bf16v(arA, 0, 16 * SEQ)
            xsbf_b = bf16v(arB, 0, 16 * SEQ)
            tA2 = [f32v(arC, 0, 2048), f32v(arC, 8192, 2048)]
            tB2 = [f32v(arC, 16384, 2048), f32v(arC, 24576, 2048)]
            tC2 = [f32v(arD, 0, 2048), f32v(arD, 8192, 2048)]
            xsf = [f32v(arD, 16384, 2048), f32v(arD, 16384, 2048)]
            resf_b = f32v(arD, 24576, 2048)
            ybf_st = [bf16v(arD, 32768, 2048), bf16v(arD, 32768, 2048)]

            WSEMS = ("s_w", "s_w1", "s_w2", "s_w3")
            for k in range(16):
                n_wdt = p.dma("qp", WSEMS[k % 4], lambda E, k=k: E.dma_start(
                    out=w_dt_sb[:, k * SEQ:(k + 1) * SEQ], in_=w_dt[k * 128:(k + 1) * 128, :]))
            for ws in WSEMS:
                p.w("te", ws, p.c[ws])

            pair_free = [("ve", 0), ("ve", 0), ("ve", 0)]
            xsf_load = 0; xsf_freeV = 0; xsf_freeP = 0
            resf_load = 0; resf_freeV = 0
            ybf_free = 0
            t_free = [0, 0]
            t_freeP = [0, 0]
            ucount = 0
            for b in range(NB):
                p.w("sy", "te", p.c["te"])
                n_xsb = p.dma("qs", "s_ld", lambda E, b=b: E.dma_start(
                    out=xsbf_b.rearrange("p (m t) -> p m t", m=16), in_=xsbf_sp[b]))
                p.w("te", "s_ld", n_xsb)
                for m in range(16):
                    buf = ucount % 2
                    tA = tA2[buf]; tB = tB2[buf]; tC = tC2[buf]
                    p.w("sy", "ve", xsf_freeV)
                    p.w("sy", "po", xsf_freeP)
                    xsf_load = p.dma("qs", "s_xf0", lambda E, b=b, m=m:
                        E.dma_start(out=xsf[0], in_=xs_sp[b, m]))
                    p.w("se", "ve", resf_freeV)
                    resf_load = p.dma("qa", "s_rs", lambda E, b=b, m=m: E.dma_start(out=resf_b, in_=res_sp[b, m]))
                    pA = PAIR[(ucount * 2) % 3]
                    pB = PAIR[(ucount * 2 + 1) % 3]
                    for pi, pp in enumerate((pA, pB)):
                        eng, val = pair_free[(ucount * 2 + pi) % 3]
                        p.w("te", eng, val)
                        for k in range(16):
                            for sl in range(2):
                                n_mm = p.te(lambda E, pp=pp, k=k, m=m, pi=pi, sl=sl:
                                    E.matmul(pp[:, sl * 512:(sl + 1) * 512],
                                             w_dt_sb[:, k * SEQ + m * 128: k * SEQ + (m + 1) * 128],
                                             xsbf_b[:, k * SEQ + pi * 1024 + sl * 512: k * SEQ + pi * 1024 + (sl + 1) * 512],
                                             start=(k == 0), stop=(k == 15)),
                                    inc=(k == 15 and sl == 1))
                    p.w("se", "te", n_mm)
                    p.w("se", "ve", t_free[buf])
                    p.w("se", "po", t_freeP[buf])
                    for pi, pp in enumerate((pA, pB)):
                        n_e1 = p.se(lambda E, pp=pp, pi=pi, m=m, tA=tA:
                            E.activation(tA[:, pi * 1024:(pi + 1) * 1024], pp[:], AF.Exp, bias=bdt_sb[:, m:m + 1]))
                    pair_free[(ucount * 2) % 3] = ("se", n_e1)
                    pair_free[(ucount * 2 + 1) % 3] = ("se", n_e1)
                    p.se(lambda E, tA=tA, tB=tB: E.activation(tB, tA, AF.Ln, bias=c_two[:]))
                    n_a = p.se(lambda E, tA=tA, tB=tB: E.activation(tA, tB, AF.Exp, bias=c_zero[:], scale=-1.0))
                    p.w("po", "se", n_a)
                    p.w("po", "s_xf0", xsf_load)
                    p.po(lambda E, tA=tA, tB=tB: E.tensor_tensor(tB, tA, xsf[0], ALU.mult))
                    n_pb = p.po(lambda E, tB=tB: E.tensor_tensor(tB, xsf[0], tB, ALU.subtract))
                    p.w("ve", "po", n_pb)
                    p.ve(lambda E, tA=tA: E.memset(tA[:, 0:1], 0.0))
                    p.ve(lambda E, tB=tB: E.tensor_copy(tB[:, 0:1], xsf[0][:, 0:1]))
                    n_scan = p.ve(lambda E, tA=tA, tB=tB, tC=tC: E.tensor_tensor_scan(tC, tA, tB, c_zero[:], ALU.mult, ALU.add))
                    n_sk = p.ve(lambda E, m=m, tC=tC:
                        E.scalar_tensor_tensor(tC, xsf[0], dsk_sb[:, m:m + 1], tC, ALU.mult, ALU.add))
                    xsf_freeV = n_sk
                    xsf_freeP = n_pb
                    p.w("ve", "s_rs", resf_load)
                    p.w("ve", "s_ybf0", ybf_free)
                    n_yb = p.ve(lambda E, tC=tC: E.tensor_tensor(ybf_st[0], tC, resf_b, ALU.mult))
                    resf_freeV = n_yb
                    t_free[buf] = n_yb
                    t_freeP[buf] = n_pb
                    p.w("sy", "ve", n_yb)
                    ybf_free = p.dma("qs", "s_ybf0", lambda E, b=b, m=m:
                        E.dma_start(out=ybf_sp[b, :, m, :], in_=ybf_st[0]))
                    ucount += 1

            if debug:
                pass
                p.dma("qs", "s_ld", lambda E: E.dma_start(out=dbg["ybf"], in_=ybf_sp))

        if "p3b" in phases:
            barrier()
            y_b = bf16v(arA, 0, 16 * SEQ)
            w_out_sb = bf16v(arB, 0, 16 * 1024)
            gt = [bf16v(arB, 32768, 1024), bf16v(arB, 34816, 1024)]
            bout_sb = f32v(arB, 36864, 1024)
            lng_sb = f32v(arB, 40960, 1024)
            lnb_sb = f32v(arB, 45056, 1024)
            xt2 = [f32v(arC, 0, 1024), f32v(arC, 4096, 1024)]
            hsb = f32v(arC, 8192, 1024)
            xnt = f32v(arC, 12288, 1024)
            tE = f32v(arC, 16384, 1024)
            ost = [f32v(arC, 20480, 1024), f32v(arC, 24576, 1024)]
            bnst2 = f32v(arC, 28672, 12)
            bnag2 = f32v(arC, 28736, 2)
            rstd2 = f32v(arC, 28744, 1)
            tln2 = f32v(arC, 28748, 1)

            WSEMS = ("s_w", "s_w1", "s_w2", "s_w3")
            for k in range(16):
                n_wo = p.dma("qp", WSEMS[k % 4], lambda E, k=k: E.dma_start(
                    out=w_out_sb[:, k * 1024:(k + 1) * 1024], in_=w_out[k * 128:(k + 1) * 128, :]))
            for ws in WSEMS:
                p.w("te", ws, p.c[ws])
            for dst2, src2 in ((bout_sb, b_out_b), (lng_sb, ln_g_b), (lnb_sb, ln_b_b)):
                n_c3 = p.dma("qs", "s_ld", lambda E, d=dst2, s=src2: E.dma_start(out=d, in_=s))
            p.w("ve", "s_ld", n_c3)

            pair_free = [("ve", 0), ("ve", 0), ("ve", 0)]
            x_load = [0, 0]; x_freeV = [0, 0]
            g_load = [0, 0]; g_freeV = [0, 0]
            o_free = [0, 0]
            ucount = 0
            for b in range(NB):
                p.w("sy", "te", p.c["te"])
                n_yb = p.dma("qs", "s_ld", lambda E, b=b: E.dma_start(
                    out=y_b.rearrange("p (m t) -> p m t", m=16), in_=ybf_sp[b]))
                p.w("te", "s_ld", n_yb)
                for tt in range(16):
                    buf = ucount % 2
                    p.w("sy", "ve", x_freeV[buf])
                    x_load[buf] = p.dma("qs", "s_x%d" % buf, lambda E, b=b, tt=tt, d=xt2[buf]:
                        E.dma_start(out=d, in_=x[b, tt * 128:(tt + 1) * 128, :]))
                    p.w("se", "ve", g_freeV[buf])
                    g_load[buf] = p.dma("qa", "s_g%d" % buf, lambda E, b=b, tt=tt, d=gt[buf]:
                        E.dma_start(out=d, in_=g_sp[b, tt]))
                    dpair = PAIR[ucount % 3]
                    eng, val = pair_free[ucount % 3]
                    p.w("te", eng, val)
                    for k in range(16):
                        for sl in range(2):
                            n_mm = p.te(lambda E, dpair=dpair, k=k, tt=tt, sl=sl:
                                E.matmul(dpair[:, sl * 512:(sl + 1) * 512],
                                         y_b[:, k * SEQ + tt * 128: k * SEQ + (tt + 1) * 128],
                                         w_out_sb[:, k * 1024 + sl * 512: k * 1024 + (sl + 1) * 512],
                                         start=(k == 0), stop=(k == 15)),
                                inc=(k == 15 and sl == 1))
                    p.w("ve", "s_x%d" % buf, x_load[buf])
                    ttg = b * 16 + tt
                    p.ve(lambda E, d=xt2[buf], ttg=ttg:
                        E.tensor_scalar(xnt, d, ln1s[:, ttg * 2:ttg * 2 + 1], ln1s[:, ttg * 2 + 1:ttg * 2 + 2],
                                        ALU.subtract, ALU.mult))
                    p.w("ve", "te", n_mm)
                    n_h1 = p.ve(lambda E, dpair=dpair: E.tensor_tensor(hsb, dpair[:], xnt, ALU.add))
                    pair_free[ucount % 3] = ("ve", n_h1)
                    p.ve(lambda E: E.tensor_tensor(hsb, hsb, bout_sb, ALU.add))
                    p.ve(lambda E: E.bn_stats(bnst2[:, 0:6], hsb[:, 0:512]))
                    p.ve(lambda E: E.bn_stats(bnst2[:, 6:12], hsb[:, 512:1024]))
                    n_ag2 = p.ve(lambda E: E.bn_aggr(bnag2, bnst2))
                    p.w("se", "ve", n_ag2)
                    p.se(lambda E: E.activation(tln2, bnag2[:, 1:2], AF.Ln, bias=c_eps[:]))
                    n_rs2 = p.se(lambda E: E.activation(rstd2, tln2, AF.Exp, bias=c_zero[:], scale=-0.5))
                    p.w("ve", "se", n_rs2)
                    p.ve(lambda E: E.tensor_scalar(tE, hsb, bnag2[:, 0:1], rstd2, ALU.subtract, ALU.mult))
                    p.ve(lambda E: E.tensor_tensor(tE, tE, lng_sb, ALU.mult))
                    p.ve(lambda E: E.tensor_tensor(tE, tE, lnb_sb, ALU.add))
                    p.w("ve", "s_g%d" % buf, g_load[buf])
                    n_sg2 = p.ve(lambda E, d=gt[buf]: E.tensor_tensor(tE, tE, d, ALU.mult))
                    g_freeV[buf] = n_sg2
                    p.w("ve", "s_o%d" % buf, o_free[buf])
                    n_out = p.ve(lambda E, d=xt2[buf], o=ost[buf]: E.tensor_tensor(o, tE, d, ALU.add))
                    x_freeV[buf] = n_out
                    p.w("sy", "ve", n_out)
                    o_free[buf] = p.dma("qs", "s_o%d" % buf, lambda E, o=ost[buf], b=b, tt=tt:
                        E.dma_start(out=out[b, tt * 128:(tt + 1) * 128, :], in_=o))
                    ucount += 1

        for kk, vv in p.c.items():
            if kk not in ("te", "ve", "se") and vv > 0:
                p.w("sy", kk, vv)

        with nc.Block() as blk:
            @blk.sync
            def _(E):
                for f in p.ops["sy"]:
                    f(E)

            @blk.tensor
            def _(E):
                for f in p.ops["te"]:
                    f(E)

            @blk.vector
            def _(E):
                for f in p.ops["ve"]:
                    f(E)

            @blk.scalar
            def _(E):
                for f in p.ops["se"]:
                    f(E)

            @blk.gpsimd
            def _(E):
                for f in p.ops["po"]:
                    f(E)

    return nc


def make_aux(inputs):
    aux = {}
    aux["b_in2"] = np.ascontiguousarray(inputs["L0_attn_b_in"].reshape(32, 128).T)
    cw4 = inputs["L0_attn_conv_w"][:, 0, :].reshape(16, 128, 4)
    aux["conv_w2"] = np.ascontiguousarray(np.transpose(cw4, (1, 0, 2)).reshape(128, 64))
    aux["conv_b2"] = np.ascontiguousarray(inputs["L0_attn_conv_b"].reshape(16, 128).T)
    aux["b_dt2"] = np.ascontiguousarray(inputs["L0_attn_b_dt"].reshape(16, 128).T)
    aux["dsk2"] = np.ascontiguousarray(inputs["L0_attn_D_skip"].reshape(16, 128).T)
    aux["b_out_b"] = np.ascontiguousarray(np.broadcast_to(inputs["L0_attn_b_out"], (128, 1024)))
    aux["ln_g_b"] = np.ascontiguousarray(np.broadcast_to(inputs["L0_attn_ln_g"], (128, 1024)))
    aux["ln_b_b"] = np.ascontiguousarray(np.broadcast_to(inputs["L0_attn_ln_b"], (128, 1024)))
    wrT = np.ascontiguousarray(inputs["L0_ffn_w_router"]).T.reshape(8, 128, 8)
    aux["wr2"] = np.ascontiguousarray(np.transpose(wrT, (1, 0, 2)).reshape(128, 64))
    aux["ident"] = np.eye(128, dtype=ml_dtypes.bfloat16)
    return aux


def make_in_maps(inputs):
    aux = make_aux(inputs)
    in_maps = []
    for c in range(8):
        m = {
            "x": np.ascontiguousarray(inputs["L0_attn_x"][c * 2:(c + 1) * 2]),
            "w_in": inputs["L0_attn_w_in"],
            "w_dt": inputs["L0_attn_w_dt"],
            "w_out": inputs["L0_attn_w_out"],
            "wg": inputs["L0_ffn_wg"], "wu": inputs["L0_ffn_wu"], "wd": inputs["L0_ffn_wd"],
            "wg_s": inputs["L0_ffn_wg_s"], "wu_s": inputs["L0_ffn_wu_s"], "wd_s": inputs["L0_ffn_wd_s"],
        }
        m.update(aux)
        in_maps.append(m)
    return in_maps


def kernel(**inputs):
    nc = build(debug=False, phases=PHASES)
    in_maps = make_in_maps(inputs)
    res = run_bass_kernel_spmd(nc, in_maps, core_ids=list(range(8)))
    return np.concatenate([r["out"] for r in res.results], axis=0)


# version 07
# speedup vs baseline: 1.1822x; 1.0842x over previous
import numpy as np
import ml_dtypes
import concourse.bass as bass
import concourse.mybir as mybir
from concourse.bass_utils import run_bass_kernel_spmd
from contextlib import ExitStack

F32 = mybir.dt.float32
BF16 = mybir.dt.bfloat16
AF = mybir.ActivationFunctionType
ALU = mybir.AluOpType
AX = mybir.AxisListType

D = 1024
DI = 2048
SEQ = 2048
NB = 2
NTT = 16
NE = 8
IE = 1408
NI = 11
EPS = 1e-5

PHASES = ("p0", "p1", "p2", "p3a", "p3b")
DEBUG = False


class Prog:
    ENGS = ("te", "ve", "se", "po", "sy")

    def __init__(self, nc, sems):
        self.nc = nc
        self.sem = sems
        self.ops = {e: [] for e in self.ENGS}
        self.c = {k: 0 for k in sems}

    def w(self, eng, sem, val):
        if val > 0:
            self.ops[eng].append(lambda E, s=self.sem[sem], v=val: E.wait_ge(s, v))

    def te(self, fn, inc=True):
        if inc:
            self.c["te"] += 1
            n = self.c["te"]
            self.ops["te"].append(lambda E, f=fn, s=self.sem["te"]: f(E).then_inc(s, 1))
            return n
        self.ops["te"].append(fn)
        return self.c["te"]

    def _serial(self, eng, fn):
        prev = self.c[eng]
        if prev > 0:
            self.ops[eng].append(lambda E, s=self.sem[eng], v=prev: E.wait_ge(s, v))
        self.c[eng] += 1
        n = self.c[eng]
        self.ops[eng].append(lambda E, f=fn, s=self.sem[eng]: f(E).then_inc(s, 1))
        return n

    def ve(self, fn):
        return self._serial("ve", fn)

    def se(self, fn):
        return self._serial("se", fn)

    def dma(self, q, sem, fn):
        eng = {"qs": "sy", "qa": "se", "qp": "po"}[q]
        prev = self.c[sem]
        if prev > 0:
            self.ops[eng].append(lambda E, s=self.sem[sem], v=prev: E.wait_ge(s, v))
        self.c[sem] += 16
        n = self.c[sem]
        self.ops[eng].append(lambda E, f=fn, s=self.sem[sem]: f(E).then_inc(s, 16))
        return n


def build(debug=DEBUG, phases=PHASES):
    nc = bass.Bass("TRN2", target_bir_lowering=False)

    x = nc.dram_tensor("x", [NB, SEQ, D], F32, kind="ExternalInput").ap()
    w_in = nc.dram_tensor("w_in", [D, 2 * DI], F32, kind="ExternalInput").ap()
    b_in2 = nc.dram_tensor("b_in2", [128, 32], F32, kind="ExternalInput").ap()
    conv_w2 = nc.dram_tensor("conv_w2", [128, 64], F32, kind="ExternalInput").ap()
    conv_b2 = nc.dram_tensor("conv_b2", [128, 16], F32, kind="ExternalInput").ap()
    w_dt = nc.dram_tensor("w_dt", [DI, DI], F32, kind="ExternalInput").ap()
    b_dt2 = nc.dram_tensor("b_dt2", [128, 16], F32, kind="ExternalInput").ap()
    dsk2 = nc.dram_tensor("dsk2", [128, 16], F32, kind="ExternalInput").ap()
    w_out = nc.dram_tensor("w_out", [DI, D], F32, kind="ExternalInput").ap()
    b_out_b = nc.dram_tensor("b_out_b", [128, D], F32, kind="ExternalInput").ap()
    ln_g_b = nc.dram_tensor("ln_g_b", [128, D], F32, kind="ExternalInput").ap()
    ln_b_b = nc.dram_tensor("ln_b_b", [128, D], F32, kind="ExternalInput").ap()
    wr2 = nc.dram_tensor("wr2", [128, 64], F32, kind="ExternalInput").ap()
    wg = nc.dram_tensor("wg", [NE, D, IE], F32, kind="ExternalInput").ap()
    wu = nc.dram_tensor("wu", [NE, D, IE], F32, kind="ExternalInput").ap()
    wd = nc.dram_tensor("wd", [NE, IE, D], F32, kind="ExternalInput").ap()
    wg_s = nc.dram_tensor("wg_s", [D, IE], F32, kind="ExternalInput").ap()
    wu_s = nc.dram_tensor("wu_s", [D, IE], F32, kind="ExternalInput").ap()
    wd_s = nc.dram_tensor("wd_s", [IE, D], F32, kind="ExternalInput").ap()
    ident = nc.dram_tensor("ident", [128, 128], BF16, kind="ExternalInput").ap()

    out = nc.dram_tensor("out", [NB, SEQ, D], F32, kind="ExternalOutput").ap()

    xnT_sp = nc.dram_tensor("xnT_sp", [NB, 128, 8, SEQ], BF16, kind="Internal").ap()
    g_sp = nc.dram_tensor("g_sp", [NB, NTT, 128, D], BF16, kind="Internal").ap()
    xs_sp = nc.dram_tensor("xs_sp", [NB, 16, 128, SEQ], F32, kind="Internal").ap()
    xsbf_sp = nc.dram_tensor("xsbf_sp", [NB, 128, 16, SEQ], BF16, kind="Internal").ap()
    res_sp = nc.dram_tensor("res_sp", [NB, 16, 128, SEQ], F32, kind="Internal").ap()
    ybf_sp = nc.dram_tensor("ybf_sp", [NB, 128, 16, SEQ], BF16, kind="Internal").ap()

    dbg = {}
    if debug:
        def dbg_t(name, shape, dt):
            dbg[name] = nc.dram_tensor("dbg_" + name, shape, dt, kind="ExternalOutput").ap()
        dbg_t("xnT", [NB, 128, 8, SEQ], BF16)
        dbg_t("cw", [128, 256], F32)
        dbg_t("g", [NB, NTT, 128, D], BF16)
        dbg_t("xs", [NB, 16, 128, SEQ], F32)
        dbg_t("res", [NB, 16, 128, SEQ], F32)
        dbg_t("ybf", [NB, 128, 16, SEQ], BF16)

    with ExitStack() as stk:
        sb = lambda name, shape, dt=F32: stk.enter_context(nc.sbuf_tensor(name, shape, dt))
        sems = {}
        DMA_SEMS = ("s_ld", "s_w", "s_w1", "s_w2", "s_w3", "s_x0", "s_x1", "s_sp0", "s_sp1",
                    "s_wg0", "s_wg1", "s_wg2", "s_wg3", "s_wd", "s_gb0", "s_gb1",
                    "s_xsf0", "s_xsf1", "s_xsb0", "s_xsb1", "s_rv0", "s_rv1",
                    "s_xf0", "s_xf1", "s_rs", "s_ybf0", "s_ybf1",
                    "s_g0", "s_g1", "s_o0", "s_o1")
        for s in ("te", "ve", "se") + DMA_SEMS:
            sems[s] = stk.enter_context(nc.semaphore("sem_" + s))
        p = Prog(nc, sems)

        pf = stk.enter_context(nc.psum_tensor("pf", [128, 3072], F32))
        pbf = [stk.enter_context(nc.psum_tensor(f"pbf{i}", [128, 1024], BF16)) for i in range(2)]
        PAIR = [pf[:, k * 1024:(k + 1) * 1024] for k in range(3)]

        arA = sb("arA", [128, 16384])
        arB = sb("arB", [128, 16384])
        arC = sb("arC", [128, 8192])
        arD = sb("arD", [128, 9728])

        def f32v(ar, off, n):
            assert off % 4 == 0
            return ar[:, off // 4: off // 4 + n]

        def bf16v(ar, off, n):
            assert off % 2 == 0
            return ar[:].bitcast(BF16)[:, off // 2: off // 2 + n]

        ident_sb = sb("ident_sb", [128, 128], BF16)
        wr_sb = sb("wr_sb", [128, 64], BF16)
        biases = sb("biases", [128, 32])
        convw_sb = sb("convw_sb", [128, 64])
        convb_sb = sb("convb_sb", [128, 16])
        bdt_sb = sb("bdt_sb", [128, 16])
        dsk_sb = sb("dsk_sb", [128, 16])
        cw_all = sb("cw_all", [128, 32 * 8])
        ln1s = sb("ln1s", [128, 64])
        c_eps = sb("c_eps", [128, 1])
        c_two = sb("c_two", [128, 1])
        c_one = sb("c_one", [128, 1])
        c_zero = sb("c_zero", [128, 1])
        ones8 = sb("ones8", [128, 8])

        for dst, src in ((ident_sb, ident), (biases, b_in2), (convw_sb, conv_w2),
                         (convb_sb, conv_b2), (bdt_sb, b_dt2), (dsk_sb, dsk2)):
            p.dma("qs", "s_ld", lambda E, d=dst, s=src: E.dma_start(out=d[:], in_=s))
        n_const = p.c["s_ld"]
        n_wr = p.dma("qp", "s_w", lambda E: E.dma_start(out=wr_sb[:], in_=wr2))
        for buf, v in ((c_eps, EPS), (c_two, 2.0), (c_one, 1.0), (c_zero, 0.0), (ones8, 1.0)):
            p.ve(lambda E, b=buf, v=v: E.memset(b[:], v))

        def barrier():
            snap = {kk: vv for kk, vv in p.c.items() if vv > 0}
            for eng in ("te", "ve", "se", "sy", "po"):
                for ss, vv in snap.items():
                    p.w(eng, ss, vv)

        if "p0" in phases:
            xtile = [f32v(arD, 0, 1024), f32v(arD, 4096, 1024)]
            xnbf = [bf16v(arD, 8192, 1024), bf16v(arD, 10240, 1024)]
            stage = [bf16v(arD, 12288, 1024), bf16v(arD, 14336, 1024)]
            bnst = f32v(arD, 16384, 12)
            bnag = f32v(arD, 16448, 2)
            rstd0 = f32v(arD, 16456, 1)
            tln = f32v(arD, 16460, 1)
            off = 16464
            smalls = {}
            for nm in ("sc8", "e8", "eq1", "cum8", "ile8", "msk1", "sv2", "eq2", "mskT", "mskd"):
                smalls[nm] = f32v(arD, off, 8); off += 32
            for nm in ("m1", "nm1", "m2", "den8", "rden"):
                smalls[nm] = f32v(arD, off, 1); off += 4
            sc8 = smalls["sc8"]; e8 = smalls["e8"]; eq1 = smalls["eq1"]; cum8 = smalls["cum8"]
            ile8 = smalls["ile8"]; msk1 = smalls["msk1"]; sv2 = smalls["sv2"]; eq2 = smalls["eq2"]
            mskT = smalls["mskT"]; mskd = smalls["mskd"]; m1 = smalls["m1"]; nm1 = smalls["nm1"]
            m2 = smalls["m2"]; den8 = smalls["den8"]; rden = smalls["rden"]

            p.w("te", "s_ld", n_const)
            p.w("te", "s_w", n_wr)

            xload = [0, 0]
            xfree = [0, 0]
            stfree = [(0, 0), (0, 0)]
            ptfree = [0, 0]
            prfree = 0
            xnbf_free = [0, 0]

            for tt in range(32):
                b, tl = divmod(tt, NTT)
                buf = tt % 2
                p.w("sy", "ve", xfree[buf])
                xload[buf] = p.dma("qs", "s_x%d" % buf, lambda E, b=b, tl=tl, d=xtile[buf]:
                    E.dma_start(out=d, in_=x[b, tl * 128:(tl + 1) * 128, :]))
                p.w("ve", "s_x%d" % buf, xload[buf])
                p.ve(lambda E, d=xtile[buf]: E.bn_stats(bnst[:, 0:6], d[:, 0:512]))
                p.ve(lambda E, d=xtile[buf]: E.bn_stats(bnst[:, 6:12], d[:, 512:1024]))
                n_ag = p.ve(lambda E: E.bn_aggr(bnag, bnst))
                p.w("se", "ve", n_ag)
                p.se(lambda E: E.activation(tln, bnag[:, 1:2], AF.Ln, bias=c_eps[:], scale=1.0))
                n_rs = p.se(lambda E: E.activation(rstd0, tln, AF.Exp, bias=c_zero[:], scale=-0.5))
                p.w("ve", "se", n_rs)
                p.ve(lambda E, tt=tt: E.tensor_copy(ln1s[:, tt * 2:tt * 2 + 1], bnag[:, 0:1]))
                p.ve(lambda E, tt=tt: E.tensor_copy(ln1s[:, tt * 2 + 1:tt * 2 + 2], rstd0))
                p.w("ve", "te", xnbf_free[buf])
                n_xn = p.ve(lambda E, d=xtile[buf], o=xnbf[buf]:
                    E.tensor_scalar(o, d, bnag[:, 0:1], rstd0, ALU.subtract, ALU.mult))
                xfree[buf] = n_xn
                p.w("te", "ve", n_xn)
                p.w("te", "ve", ptfree[buf])
                for dd in range(8):
                    n_tr = p.te(lambda E, dd=dd, o=pbf[buf], i=xnbf[buf]:
                        E.transpose(o[:, dd * 128:(dd + 1) * 128], i[:, dd * 128:(dd + 1) * 128], ident_sb[:]),
                        inc=(dd == 7))
                xnbf_free[buf] = n_tr
                p.w("ve", "te", n_tr)
                p.w("ve", "s_sp%d" % buf, stfree[buf][0])
                p.w("ve", "te", stfree[buf][1])
                n_ev = p.ve(lambda E, o=stage[buf], i=pbf[buf]: E.tensor_copy(o, i[:]))
                ptfree[buf] = n_ev
                p.w("se", "ve", n_ev)
                n_sp = p.dma("qa", "s_sp%d" % buf, lambda E, b=b, tl=tl, i=stage[buf]:
                    E.dma_start(out=xnT_sp[b, :, :, tl * 128:(tl + 1) * 128],
                                in_=i.rearrange("p (d c) -> p d c", d=8)))
                p.w("te", "ve", n_ev)
                p.w("te", "ve", prfree)
                for dd in range(8):
                    n_rt = p.te(lambda E, dd=dd, i=stage[buf]:
                        E.matmul(PAIR[0][:, 0:8], i[:, dd * 128:(dd + 1) * 128], wr_sb[:, dd * 8:(dd + 1) * 8],
                                 start=(dd == 0), stop=(dd == 7)), inc=(dd == 7))
                stfree[buf] = (n_sp, n_rt)
                p.w("ve", "te", n_rt)
                n_cp = p.ve(lambda E: E.tensor_copy(sc8, PAIR[0][:, 0:8]))
                prfree = n_cp
                p.ve(lambda E: E.tensor_reduce(m1, sc8, AX.X, ALU.max))
                n_m1 = p.ve(lambda E: E.tensor_scalar(nm1, m1, -1.0, None, ALU.mult))
                p.w("se", "ve", n_m1)
                n_e8 = p.se(lambda E: E.activation(e8, sc8, AF.Exp, bias=nm1, scale=1.0))
                p.ve(lambda E: E.tensor_scalar(eq1, sc8, m1, None, ALU.is_ge))
                p.ve(lambda E: E.tensor_tensor_scan(cum8, ones8[:], eq1, c_zero[:], ALU.mult, ALU.add))
                p.ve(lambda E: E.tensor_scalar(ile8, cum8, 1.5, None, ALU.is_le))
                p.ve(lambda E: E.tensor_tensor(msk1, eq1, ile8, ALU.mult))
                p.ve(lambda E: E.scalar_tensor_tensor(sv2, msk1, -1e30, sc8, ALU.mult, ALU.add))
                p.ve(lambda E: E.tensor_reduce(m2, sv2, AX.X, ALU.max))
                p.ve(lambda E: E.tensor_scalar(eq2, sv2, m2, None, ALU.is_ge))
                p.ve(lambda E: E.tensor_tensor_scan(cum8, ones8[:], eq2, c_zero[:], ALU.mult, ALU.add))
                p.ve(lambda E: E.tensor_scalar(ile8, cum8, 1.5, None, ALU.is_le))
                p.ve(lambda E: E.tensor_tensor(mskT, eq2, ile8, ALU.mult))
                p.ve(lambda E: E.tensor_tensor(mskT, mskT, msk1, ALU.add))
                p.w("ve", "se", n_e8)
                p.ve(lambda E: E.tensor_tensor(mskd, mskT, e8, ALU.mult))
                p.ve(lambda E: E.tensor_reduce(den8, mskd, AX.X, ALU.add))
                p.ve(lambda E: E.reciprocal(rden, den8))
                p.ve(lambda E, tt=tt: E.tensor_scalar(cw_all[:, tt * 8:(tt + 1) * 8], mskd, rden, None, ALU.mult))

            if debug:
                pass
                p.w("sy", "ve", p.c["ve"])
                p.dma("qs", "s_ld", lambda E: E.dma_start(out=dbg["xnT"], in_=xnT_sp))
                p.dma("qs", "s_ld", lambda E: E.dma_start(out=dbg["cw"], in_=cw_all[:]))

        if "p1" in phases:
            barrier()
            y_acc = arA[:, :]
            h_e = bf16v(arB, 0, NI * SEQ)
            xn_b = bf16v(arC, 0, 8 * SEQ)
            wd_sb = bf16v(arD, 0, NI * 1024)
            wgu = bf16v(arD, 22528, 8 * 1024)
            dn_tmp = [f32v(arB, 45056, 1024), f32v(arB, 49152, 1024)]
            gbf = [bf16v(arB, 53248, 1024), bf16v(arB, 55296, 1024)]
            sg_tmp = [f32v(arB, 57344, 1024), f32v(arB, 61440, 1024)]

            E_ORDER = [8] + list(range(8))

            def wgu_src(e, g, which, nih):
                t = (wg_s if which == 0 else wu_s) if e == 8 else (wg if which == 0 else wu)[e]
                return t[:, g * 256:g * 256 + nih * 128].rearrange("(k p) (ih m) -> p k ih m", p=128, m=128)

            def wd_src(e):
                t = wd_s if e == 8 else wd[e]
                return t.rearrange("(it p) f -> p it f", p=128)

            pair_free = [("ve", 0), ("ve", 0), ("ve", 0)]
            slot_load = [0, 0, 0, 0]
            slot_freeT = [0, 0, 0, 0]
            wd_load = 0
            wd_freeT = 0
            n_last_dn_mm = 0
            n_last_h_ev = 0
            n_se_shared = 0
            sg_free = [0, 0]
            dn_free = [0, 0]
            gb_free = [0, 0]
            units = [(e, i) for e in E_ORDER for i in range(NI)]
            NU = len(units)

            for b in range(NB):
                p.w("sy", "te", p.c["te"])
                n_xnb = p.dma("qs", "s_ld", lambda E, b=b: E.dma_start(
                    out=xn_b.rearrange("p (d t) -> p d t", d=8), in_=xnT_sp[b]))
                p.w("te", "s_ld", n_xnb)
                first_v_add_waits = 0

                GROUPS = [(e_, gg) for e_ in E_ORDER for gg in range(6)]

                def issue_wgu_dma(gidx):
                    e, g = GROUPS[gidx]
                    nih = 2 if g < 5 else 1
                    for which in (0, 1):
                        slot = (gidx * 2 + which) % 4
                        p.w("po", "te", slot_freeT[slot])
                        slot_load[slot] = p.dma("qp", "s_wg%d" % slot, lambda E, e=e, g=g, w=which, s=slot, nih=nih:
                            E.dma_start(out=wgu[:, s * 2048:(s + 1) * 2048].rearrange(
                                            "p (k ih m) -> p k ih m", k=8, ih=2)[:, :, 0:nih, :],
                                        in_=wgu_src(e, g, w, nih)))

                issue_wgu_dma(0)
                issue_wgu_dma(1)

                ucount = 0
                for uidx, (e, i) in enumerate(units):
                    eidx = uidx // NI
                    gidx = eidx * 6 + i // 2
                    ih = i % 2
                    last_of_group = (ih == 1) or (i == NI - 1)
                    for th in range(2):
                        gp = PAIR[(ucount * 2) % 3]
                        up = PAIR[(ucount * 2 + 1) % 3]
                        gslot = (gidx * 2) % 4
                        uslot = (gidx * 2 + 1) % 4
                        if th == 0 and ih == 0:
                            p.w("te", "s_wg%d" % gslot, slot_load[gslot])
                            p.w("te", "s_wg%d" % uslot, slot_load[uslot])
                        for pi, (pp, slot) in enumerate(((gp, gslot), (up, uslot))):
                            eng, val = pair_free[(ucount * 2 + pi) % 3]
                            p.w("te", eng, val)
                            for k in range(8):
                                for sl in range(2):
                                    n_mm = p.te(lambda E, pp=pp, slot=slot, k=k, sl=sl, th=th, ih=ih:
                                        E.matmul(pp[:, sl * 512:(sl + 1) * 512],
                                                 wgu[:, slot * 2048 + k * 256 + ih * 128: slot * 2048 + k * 256 + ih * 128 + 128],
                                                 xn_b[:, k * SEQ + th * 1024 + sl * 512: k * SEQ + th * 1024 + (sl + 1) * 512],
                                                 start=(k == 0), stop=(k == 7)),
                                        inc=(k == 7 and sl == 1))
                        if th == 1 and last_of_group:
                            slot_freeT[gslot] = n_mm
                            slot_freeT[uslot] = n_mm
                        stb = sg_tmp[ucount % 2]
                        p.w("se", "te", n_mm)
                        p.w("se", "ve", sg_free[ucount % 2])
                        n_sg = p.se(lambda E, pp=gp, o=stb: E.activation(o, pp[:], AF.Silu))
                        p.w("ve", "se", n_sg)
                        p.w("ve", "te", n_last_dn_mm)
                        n_h = p.ve(lambda E, pp=up, o=h_e, i=i, th=th, stb=stb:
                            E.tensor_tensor(o[:, i * SEQ + th * 1024: i * SEQ + (th + 1) * 1024],
                                            stb, pp[:], ALU.mult))
                        pair_free[(ucount * 2) % 3] = ("se", n_sg)
                        pair_free[(ucount * 2 + 1) % 3] = ("ve", n_h)
                        sg_free[ucount % 2] = n_h
                        n_last_h_ev = n_h
                        ucount += 1
                    if last_of_group and gidx + 2 < len(GROUPS):
                        issue_wgu_dma(gidx + 2)
                    if i == 4:
                        p.w("po", "te", wd_freeT)
                        wd_load = p.dma("qp", "s_wd", lambda E, e=e:
                            E.dma_start(out=wd_sb.rearrange("p (it f) -> p it f", f=1024), in_=wd_src(e)))
                    if i == NI - 1:
                        p.w("te", "s_wd", wd_load)
                        p.w("te", "ve", n_last_h_ev)
                        for tt in range(16):
                            dpair = PAIR[ucount % 3]
                            eng, val = pair_free[ucount % 3]
                            p.w("te", eng, val)
                            for ii in range(NI):
                                for fh in range(2):
                                    n_dmm = p.te(lambda E, dpair=dpair, ii=ii, tt=tt, fh=fh:
                                        E.matmul(dpair[:, fh * 512:fh * 512 + 512],
                                                 h_e[:, ii * SEQ + tt * 128: ii * SEQ + (tt + 1) * 128],
                                                 wd_sb[:, ii * 1024 + fh * 512: ii * 1024 + fh * 512 + 512],
                                                 start=(ii == 0), stop=(ii == NI - 1)),
                                        inc=(ii == NI - 1 and fh == 1))
                            wd_freeT = n_dmm
                            n_last_dn_mm = n_dmm
                            ttg = b * 16 + tt
                            scale_ap = c_one[:] if e == 8 else cw_all[:, ttg * 8 + e: ttg * 8 + e + 1]
                            p.w("se", "te", n_dmm)
                            if e == 8:
                                n_dn = p.se(lambda E, dpair=dpair, tt=tt:
                                    E.activation(y_acc[:, tt * 1024: (tt + 1) * 1024],
                                                 dpair[:], AF.Copy))
                                pair_free[ucount % 3] = ("se", n_dn)
                                n_se_shared = n_dn
                            else:
                                dtb = dn_tmp[tt % 2]
                                p.w("se", "ve", dn_free[tt % 2])
                                n_dn = p.se(lambda E, dpair=dpair, o=dtb, s=scale_ap:
                                    E.activation(o, dpair[:], AF.Copy, scale=s))
                                pair_free[ucount % 3] = ("se", n_dn)
                                p.w("ve", "se", n_dn)
                                if first_v_add_waits == 0:
                                    p.w("ve", "se", n_se_shared)
                                    first_v_add_waits = 1
                                dn_free[tt % 2] = p.ve(lambda E, tt=tt, dtb=dtb:
                                    E.tensor_tensor(y_acc[:, tt * 1024: (tt + 1) * 1024],
                                                    y_acc[:, tt * 1024: (tt + 1) * 1024],
                                                    dtb, ALU.add))
                            ucount += 1
                n_vdone = p.c["ve"]
                p.w("se", "ve", n_vdone)
                for tt in range(16):
                    gb = gbf[tt % 2]
                    p.w("se", "s_gb%d" % (tt % 2), gb_free[tt % 2])
                    n_gb = p.se(lambda E, gb=gb, tt=tt:
                        E.activation(gb, y_acc[:, tt * 1024:(tt + 1) * 1024], AF.Silu))
                    p.w("se", "se", n_gb)
                    gb_free[tt % 2] = p.dma("qa", "s_gb%d" % (tt % 2), lambda E, gb=gb, b=b, tt=tt:
                        E.dma_start(out=g_sp[b, tt], in_=gb))

            if debug:
                pass
                p.dma("qs", "s_ld", lambda E: E.dma_start(out=dbg["g"], in_=g_sp))

        if "p2" in phases:
            barrier()
            w_in_sb = bf16v(arA, 0, 8 * 4096)
            xn_b = bf16v(arC, 0, 8 * SEQ)
            cpad = [f32v(arB, 0, 2052), f32v(arB, 8208, 2052)]
            xsa = [f32v(arB, 16416, 2048), f32v(arB, 24608, 2048)]
            rev = [f32v(arB, 32800, 2048), f32v(arB, 40992, 2048)]
            xsf_st = [f32v(arD, 0, 2048), f32v(arD, 8192, 2048)]
            xsbf_st = [bf16v(arD, 16384, 2048), bf16v(arD, 20480, 2048)]

            WSEMS = ("s_w", "s_w1", "s_w2", "s_w3")
            for k in range(8):
                n_win = p.dma("qp", WSEMS[k % 4], lambda E, k=k: E.dma_start(
                    out=w_in_sb[:, k * 4096:(k + 1) * 4096], in_=w_in[k * 128:(k + 1) * 128, :]))
            for ws in WSEMS:
                p.w("te", ws, p.c[ws])
            for i in range(2):
                p.ve(lambda E, i=i: E.memset(cpad[i][:, 0:3], 0.0))
            n_padz = p.c["ve"]
            p.w("se", "ve", n_padz)

            pair_free = [("ve", 0), ("ve", 0), ("ve", 0)]
            cpad_free = [0, 0]
            xsa_free = [0, 0]
            xsf_free = [0, 0]
            xsfV_free = [0, 0]
            xsbf_free = [0, 0]
            rev_free = [0, 0]
            ucount = 0
            for b in range(NB):
                p.w("sy", "te", p.c["te"])
                n_xnb = p.dma("qs", "s_ld", lambda E, b=b: E.dma_start(
                    out=xn_b.rearrange("p (d t) -> p d t", d=8), in_=xnT_sp[b]))
                p.w("te", "s_ld", n_xnb)
                for m in range(32):
                    buf = ucount % 2
                    pA = PAIR[(ucount * 2) % 3]
                    pB = PAIR[(ucount * 2 + 1) % 3]
                    for pi, pp in enumerate((pA, pB)):
                        eng, val = pair_free[(ucount * 2 + pi) % 3]
                        p.w("te", eng, val)
                        for k in range(8):
                            for sl in range(2):
                                n_mm = p.te(lambda E, pp=pp, k=k, m=m, pi=pi, sl=sl:
                                    E.matmul(pp[:, sl * 512:(sl + 1) * 512],
                                             w_in_sb[:, k * 4096 + m * 128: k * 4096 + (m + 1) * 128],
                                             xn_b[:, k * SEQ + pi * 1024 + sl * 512: k * SEQ + pi * 1024 + (sl + 1) * 512],
                                             start=(k == 0), stop=(k == 7)),
                                    inc=(k == 7 and sl == 1))
                    p.w("se", "te", n_mm)
                    if m < 16:
                        p.w("se", "ve", cpad_free[buf])
                        for pi, pp in enumerate((pA, pB)):
                            n_ev = p.se(lambda E, pp=pp, o=cpad[buf], pi=pi, m=m:
                                E.activation(o[:, 3 + pi * 1024: 3 + (pi + 1) * 1024], pp[:],
                                             AF.Silu, bias=biases[:, m:m + 1]))
                        pair_free[(ucount * 2) % 3] = ("se", n_ev)
                        pair_free[(ucount * 2 + 1) % 3] = ("se", n_ev)
                        p.w("ve", "se", n_ev)
                        p.w("ve", "se", xsa_free[buf])
                        p.ve(lambda E, o=xsa[buf], i=cpad[buf], m=m:
                            E.tensor_scalar(o, i[:, 3:3 + SEQ], convw_sb[:, m * 4 + 3: m * 4 + 4], None, ALU.mult))
                        for dd in range(1, 4):
                            n_cv = p.ve(lambda E, o=xsa[buf], i=cpad[buf], m=m, dd=dd:
                                E.scalar_tensor_tensor(o, i[:, 3 - dd: 3 - dd + SEQ],
                                                       convw_sb[:, m * 4 + 3 - dd: m * 4 + 4 - dd],
                                                       o, ALU.mult, ALU.add))
                        cpad_free[buf] = n_cv
                        p.w("se", "ve", n_cv)
                        p.w("se", "s_xsf%d" % buf, xsf_free[buf])
                        p.w("se", "ve", xsfV_free[buf])
                        n_xs = p.se(lambda E, o=xsf_st[buf], i=xsa[buf], m=m:
                            E.activation(o, i, AF.Silu, bias=convb_sb[:, m:m + 1]))
                        xsa_free[buf] = n_xs
                        p.w("ve", "se", n_xs)
                        p.w("ve", "s_xsb%d" % buf, xsbf_free[buf])
                        n_xbf = p.ve(lambda E, o=xsbf_st[buf], i=xsf_st[buf]: E.tensor_copy(o, i))
                        xsfV_free[buf] = n_xbf
                        p.w("sy", "se", n_xs)
                        xsf_free[buf] = p.dma("qs", "s_xsf%d" % buf, lambda E, i=xsf_st[buf], b=b, m=m:
                            E.dma_start(out=xs_sp[b, m], in_=i))
                        p.w("se", "ve", n_xbf)
                        xsbf_free[buf] = p.dma("qa", "s_xsb%d" % buf, lambda E, i=xsbf_st[buf], b=b, m=m:
                            E.dma_start(out=xsbf_sp[b, :, m, :], in_=i))
                    else:
                        p.w("se", "s_rv%d" % buf, rev_free[buf])
                        for pi, pp in enumerate((pA, pB)):
                            n_ev = p.se(lambda E, pp=pp, o=rev[buf], pi=pi, m=m:
                                E.activation(o[:, pi * 1024:(pi + 1) * 1024], pp[:],
                                             AF.Silu, bias=biases[:, m:m + 1]))
                        pair_free[(ucount * 2) % 3] = ("se", n_ev)
                        pair_free[(ucount * 2 + 1) % 3] = ("se", n_ev)
                        p.w("sy", "se", n_ev)
                        rev_free[buf] = p.dma("qs", "s_rv%d" % buf, lambda E, i=rev[buf], b=b, m=m:
                            E.dma_start(out=res_sp[b, m - 16], in_=i))
                    ucount += 1

            if debug:
                pass
                pass
                p.dma("qs", "s_ld", lambda E: E.dma_start(out=dbg["xs"], in_=xs_sp))
                p.dma("qs", "s_ld", lambda E: E.dma_start(out=dbg["res"], in_=res_sp))

        if "p3a" in phases:
            barrier()
            w_dt_sb = bf16v(arA, 0, 16 * SEQ)
            xsbf_b = bf16v(arB, 0, 16 * SEQ)
            tA2 = [f32v(arC, 0, 2048), f32v(arC, 8192, 2048)]
            tB2 = [f32v(arC, 16384, 2048), f32v(arC, 24576, 2048)]
            tC2 = [f32v(arD, 0, 2048), f32v(arD, 8192, 2048)]
            xsf = [f32v(arD, 16384, 2048), f32v(arD, 16384, 2048)]
            resf_b = f32v(arD, 24576, 2048)
            ybf_st = [bf16v(arD, 32768, 2048), bf16v(arD, 32768, 2048)]

            WSEMS = ("s_w", "s_w1", "s_w2", "s_w3")
            for k in range(16):
                n_wdt = p.dma("qp", WSEMS[k % 4], lambda E, k=k: E.dma_start(
                    out=w_dt_sb[:, k * SEQ:(k + 1) * SEQ], in_=w_dt[k * 128:(k + 1) * 128, :]))
            for ws in WSEMS:
                p.w("te", ws, p.c[ws])

            pair_free = [("ve", 0), ("ve", 0), ("ve", 0)]
            xsf_load = 0; xsf_freeV = 0
            resf_load = 0; resf_freeV = 0
            ybf_free = 0
            t_free = [0, 0]
            ucount = 0
            for b in range(NB):
                p.w("sy", "te", p.c["te"])
                n_xsb = p.dma("qs", "s_ld", lambda E, b=b: E.dma_start(
                    out=xsbf_b.rearrange("p (m t) -> p m t", m=16), in_=xsbf_sp[b]))
                p.w("te", "s_ld", n_xsb)
                for m in range(16):
                    buf = ucount % 2
                    tA = tA2[buf]; tB = tB2[buf]; tC = tC2[buf]
                    p.w("sy", "ve", xsf_freeV)
                    xsf_load = p.dma("qs", "s_xf0", lambda E, b=b, m=m:
                        E.dma_start(out=xsf[0], in_=xs_sp[b, m]))
                    p.w("se", "ve", resf_freeV)
                    resf_load = p.dma("qa", "s_rs", lambda E, b=b, m=m: E.dma_start(out=resf_b, in_=res_sp[b, m]))
                    pA = PAIR[(ucount * 2) % 3]
                    pB = PAIR[(ucount * 2 + 1) % 3]
                    for pi, pp in enumerate((pA, pB)):
                        eng, val = pair_free[(ucount * 2 + pi) % 3]
                        p.w("te", eng, val)
                        for k in range(16):
                            for sl in range(2):
                                n_mm = p.te(lambda E, pp=pp, k=k, m=m, pi=pi, sl=sl:
                                    E.matmul(pp[:, sl * 512:(sl + 1) * 512],
                                             w_dt_sb[:, k * SEQ + m * 128: k * SEQ + (m + 1) * 128],
                                             xsbf_b[:, k * SEQ + pi * 1024 + sl * 512: k * SEQ + pi * 1024 + (sl + 1) * 512],
                                             start=(k == 0), stop=(k == 15)),
                                    inc=(k == 15 and sl == 1))
                    p.w("se", "te", n_mm)
                    p.w("se", "ve", t_free[buf])
                    for pi, pp in enumerate((pA, pB)):
                        n_e1 = p.se(lambda E, pp=pp, pi=pi, m=m, tA=tA:
                            E.activation(tA[:, pi * 1024:(pi + 1) * 1024], pp[:], AF.Exp, bias=bdt_sb[:, m:m + 1]))
                    pair_free[(ucount * 2) % 3] = ("se", n_e1)
                    pair_free[(ucount * 2 + 1) % 3] = ("se", n_e1)
                    p.se(lambda E, tA=tA, tB=tB: E.activation(tB, tA, AF.Ln, bias=c_two[:]))
                    n_a = p.se(lambda E, tA=tA, tB=tB: E.activation(tA, tB, AF.Exp, bias=c_zero[:], scale=-1.0))
                    p.w("ve", "se", n_a)
                    p.w("ve", "s_xf0", xsf_load)
                    p.ve(lambda E, tA=tA, tB=tB: E.tensor_tensor(tB, tA, xsf[0], ALU.mult))
                    p.ve(lambda E, tB=tB: E.tensor_tensor(tB, xsf[0], tB, ALU.subtract))
                    p.ve(lambda E, tA=tA: E.memset(tA[:, 0:1], 0.0))
                    p.ve(lambda E, tB=tB: E.tensor_copy(tB[:, 0:1], xsf[0][:, 0:1]))
                    n_scan = p.ve(lambda E, tA=tA, tB=tB, tC=tC: E.tensor_tensor_scan(tC, tA, tB, c_zero[:], ALU.mult, ALU.add))
                    n_sk = p.ve(lambda E, m=m, tC=tC:
                        E.scalar_tensor_tensor(tC, xsf[0], dsk_sb[:, m:m + 1], tC, ALU.mult, ALU.add))
                    xsf_freeV = n_sk
                    p.w("ve", "s_rs", resf_load)
                    p.w("ve", "s_ybf0", ybf_free)
                    n_yb = p.ve(lambda E, tC=tC: E.tensor_tensor(ybf_st[0], tC, resf_b, ALU.mult))
                    resf_freeV = n_yb
                    t_free[buf] = n_yb
                    p.w("sy", "ve", n_yb)
                    ybf_free = p.dma("qs", "s_ybf0", lambda E, b=b, m=m:
                        E.dma_start(out=ybf_sp[b, :, m, :], in_=ybf_st[0]))
                    ucount += 1

            if debug:
                pass
                p.dma("qs", "s_ld", lambda E: E.dma_start(out=dbg["ybf"], in_=ybf_sp))

        if "p3b" in phases:
            barrier()
            y_b = bf16v(arA, 0, 16 * SEQ)
            w_out_sb = bf16v(arB, 0, 16 * 1024)
            gt = [bf16v(arB, 32768, 1024), bf16v(arB, 34816, 1024)]
            bout_sb = f32v(arB, 36864, 1024)
            lng_sb = f32v(arB, 40960, 1024)
            lnb_sb = f32v(arB, 45056, 1024)
            xt2 = [f32v(arC, 0, 1024), f32v(arC, 4096, 1024)]
            hsb = f32v(arC, 8192, 1024)
            xnt = f32v(arC, 12288, 1024)
            tE = f32v(arC, 16384, 1024)
            ost = [f32v(arC, 20480, 1024), f32v(arC, 24576, 1024)]
            bnst2 = f32v(arC, 28672, 12)
            bnag2 = f32v(arC, 28736, 2)
            rstd2 = f32v(arC, 28744, 1)
            tln2 = f32v(arC, 28748, 1)

            WSEMS = ("s_w", "s_w1", "s_w2", "s_w3")
            for k in range(16):
                n_wo = p.dma("qp", WSEMS[k % 4], lambda E, k=k: E.dma_start(
                    out=w_out_sb[:, k * 1024:(k + 1) * 1024], in_=w_out[k * 128:(k + 1) * 128, :]))
            for ws in WSEMS:
                p.w("te", ws, p.c[ws])
            for dst2, src2 in ((bout_sb, b_out_b), (lng_sb, ln_g_b), (lnb_sb, ln_b_b)):
                n_c3 = p.dma("qs", "s_ld", lambda E, d=dst2, s=src2: E.dma_start(out=d, in_=s))
            p.w("ve", "s_ld", n_c3)

            pair_free = [("ve", 0), ("ve", 0), ("ve", 0)]
            x_load = [0, 0]; x_freeV = [0, 0]
            g_load = [0, 0]; g_freeV = [0, 0]
            o_free = [0, 0]
            ucount = 0
            for b in range(NB):
                p.w("sy", "te", p.c["te"])
                n_yb = p.dma("qs", "s_ld", lambda E, b=b: E.dma_start(
                    out=y_b.rearrange("p (m t) -> p m t", m=16), in_=ybf_sp[b]))
                p.w("te", "s_ld", n_yb)
                for tt in range(16):
                    buf = ucount % 2
                    p.w("sy", "ve", x_freeV[buf])
                    x_load[buf] = p.dma("qs", "s_x%d" % buf, lambda E, b=b, tt=tt, d=xt2[buf]:
                        E.dma_start(out=d, in_=x[b, tt * 128:(tt + 1) * 128, :]))
                    p.w("se", "ve", g_freeV[buf])
                    g_load[buf] = p.dma("qa", "s_g%d" % buf, lambda E, b=b, tt=tt, d=gt[buf]:
                        E.dma_start(out=d, in_=g_sp[b, tt]))
                    dpair = PAIR[ucount % 3]
                    eng, val = pair_free[ucount % 3]
                    p.w("te", eng, val)
                    for k in range(16):
                        for sl in range(2):
                            n_mm = p.te(lambda E, dpair=dpair, k=k, tt=tt, sl=sl:
                                E.matmul(dpair[:, sl * 512:(sl + 1) * 512],
                                         y_b[:, k * SEQ + tt * 128: k * SEQ + (tt + 1) * 128],
                                         w_out_sb[:, k * 1024 + sl * 512: k * 1024 + (sl + 1) * 512],
                                         start=(k == 0), stop=(k == 15)),
                                inc=(k == 15 and sl == 1))
                    p.w("ve", "s_x%d" % buf, x_load[buf])
                    ttg = b * 16 + tt
                    p.ve(lambda E, d=xt2[buf], ttg=ttg:
                        E.tensor_scalar(xnt, d, ln1s[:, ttg * 2:ttg * 2 + 1], ln1s[:, ttg * 2 + 1:ttg * 2 + 2],
                                        ALU.subtract, ALU.mult))
                    p.w("ve", "te", n_mm)
                    n_h1 = p.ve(lambda E, dpair=dpair: E.tensor_tensor(hsb, dpair[:], xnt, ALU.add))
                    pair_free[ucount % 3] = ("ve", n_h1)
                    p.ve(lambda E: E.tensor_tensor(hsb, hsb, bout_sb, ALU.add))
                    p.ve(lambda E: E.bn_stats(bnst2[:, 0:6], hsb[:, 0:512]))
                    p.ve(lambda E: E.bn_stats(bnst2[:, 6:12], hsb[:, 512:1024]))
                    n_ag2 = p.ve(lambda E: E.bn_aggr(bnag2, bnst2))
                    p.w("se", "ve", n_ag2)
                    p.se(lambda E: E.activation(tln2, bnag2[:, 1:2], AF.Ln, bias=c_eps[:]))
                    n_rs2 = p.se(lambda E: E.activation(rstd2, tln2, AF.Exp, bias=c_zero[:], scale=-0.5))
                    p.w("ve", "se", n_rs2)
                    p.ve(lambda E: E.tensor_scalar(tE, hsb, bnag2[:, 0:1], rstd2, ALU.subtract, ALU.mult))
                    p.ve(lambda E: E.tensor_tensor(tE, tE, lng_sb, ALU.mult))
                    p.ve(lambda E: E.tensor_tensor(tE, tE, lnb_sb, ALU.add))
                    p.w("ve", "s_g%d" % buf, g_load[buf])
                    n_sg2 = p.ve(lambda E, d=gt[buf]: E.tensor_tensor(tE, tE, d, ALU.mult))
                    g_freeV[buf] = n_sg2
                    p.w("ve", "s_o%d" % buf, o_free[buf])
                    n_out = p.ve(lambda E, d=xt2[buf], o=ost[buf]: E.tensor_tensor(o, tE, d, ALU.add))
                    x_freeV[buf] = n_out
                    p.w("sy", "ve", n_out)
                    o_free[buf] = p.dma("qs", "s_o%d" % buf, lambda E, o=ost[buf], b=b, tt=tt:
                        E.dma_start(out=out[b, tt * 128:(tt + 1) * 128, :], in_=o))
                    ucount += 1

        for kk, vv in p.c.items():
            if kk not in ("te", "ve", "se") and vv > 0:
                p.w("sy", kk, vv)

        with nc.Block() as blk:
            @blk.sync
            def _(E):
                for f in p.ops["sy"]:
                    f(E)

            @blk.tensor
            def _(E):
                for f in p.ops["te"]:
                    f(E)

            @blk.vector
            def _(E):
                for f in p.ops["ve"]:
                    f(E)

            @blk.scalar
            def _(E):
                for f in p.ops["se"]:
                    f(E)

            @blk.gpsimd
            def _(E):
                for f in p.ops["po"]:
                    f(E)

    return nc


def make_aux(inputs):
    aux = {}
    aux["b_in2"] = np.ascontiguousarray(inputs["L0_attn_b_in"].reshape(32, 128).T)
    cw4 = inputs["L0_attn_conv_w"][:, 0, :].reshape(16, 128, 4)
    aux["conv_w2"] = np.ascontiguousarray(np.transpose(cw4, (1, 0, 2)).reshape(128, 64))
    aux["conv_b2"] = np.ascontiguousarray(inputs["L0_attn_conv_b"].reshape(16, 128).T)
    aux["b_dt2"] = np.ascontiguousarray(inputs["L0_attn_b_dt"].reshape(16, 128).T)
    aux["dsk2"] = np.ascontiguousarray(inputs["L0_attn_D_skip"].reshape(16, 128).T)
    aux["b_out_b"] = np.ascontiguousarray(np.broadcast_to(inputs["L0_attn_b_out"], (128, 1024)))
    aux["ln_g_b"] = np.ascontiguousarray(np.broadcast_to(inputs["L0_attn_ln_g"], (128, 1024)))
    aux["ln_b_b"] = np.ascontiguousarray(np.broadcast_to(inputs["L0_attn_ln_b"], (128, 1024)))
    wrT = np.ascontiguousarray(inputs["L0_ffn_w_router"]).T.reshape(8, 128, 8)
    aux["wr2"] = np.ascontiguousarray(np.transpose(wrT, (1, 0, 2)).reshape(128, 64))
    aux["ident"] = np.eye(128, dtype=ml_dtypes.bfloat16)
    return aux


def make_in_maps(inputs):
    aux = make_aux(inputs)
    in_maps = []
    for c in range(8):
        m = {
            "x": np.ascontiguousarray(inputs["L0_attn_x"][c * 2:(c + 1) * 2]),
            "w_in": inputs["L0_attn_w_in"],
            "w_dt": inputs["L0_attn_w_dt"],
            "w_out": inputs["L0_attn_w_out"],
            "wg": inputs["L0_ffn_wg"], "wu": inputs["L0_ffn_wu"], "wd": inputs["L0_ffn_wd"],
            "wg_s": inputs["L0_ffn_wg_s"], "wu_s": inputs["L0_ffn_wu_s"], "wd_s": inputs["L0_ffn_wd_s"],
        }
        m.update(aux)
        in_maps.append(m)
    return in_maps


def kernel(**inputs):
    nc = build(debug=False, phases=PHASES)
    in_maps = make_in_maps(inputs)
    res = run_bass_kernel_spmd(nc, in_maps, core_ids=list(range(8)))
    return np.concatenate([r["out"] for r in res.results], axis=0)
